# version 16
# baseline (speedup 1.0000x reference)
import contextlib
import numpy as np
import concourse.bass as bass
import concourse.mybir as mybir
from concourse.bass_utils import run_bass_kernel_spmd

F32 = mybir.dt.float32
BF16 = mybir.dt.bfloat16
OP = mybir.AluOpType
AF = mybir.ActivationFunctionType
AX = mybir.AxisListType

T = 4096
D = 1024
DH = 512
N = 16
HD = 64
ED = 4096
TH = 2048
TC = 512
NCH = 8
NCHO = 4
G8 = 8
G4 = 4
RINGM = 3
NKT = 32
GROUPS = [[0, 1], [2, 3], [4, 5], [6, 7]]
ARN = 42560

O_XS = 0
O_XNA = 2060
O_XNB = 2572
O_XBF = 3084
O_LN1 = 5132
O_H = 7180
O_U = 15372
O_REP = 19468
O_XCRR = 27660
O_GTRR = 30732
WREG = 33804
O_WIN = WREG
O_XAF = 0
O_Z = 16384
O_QKV = 32768
O_QKVV = 0
O_OW = 32768
O_UP0 = 32768
O_DN0 = 40960
O_DN1 = 41984
O_GW = 42496
O_RATT = 0
O_RMS1C = 8192
O_Q2 = 10240
O_C1M = 14336
O_KP = 8192
O_VT0 = 10240
O_VT1 = 11280
O_QT = 12320
O_P0 = 12576
O_P1 = 13088
O_AV = 13600
O_AVN = 14112
O_XAFC = 0
O_HID = 4096
O_RMS2C = 12288
O_FROW = 14336


class Emitter:
    def __init__(self, nc, stack):
        self.nc = nc
        self.sems = {}
        self.counts = {}
        self.seen = {}
        self.stack = stack

    def sem(self, name):
        self.sems[name] = self.stack.enter_context(self.nc.semaphore(name))
        self.counts[name] = 0

    def inc(self, instr, name, k=1):
        instr.then_inc(self.sems[name], k)
        self.counts[name] += k
        return self.counts[name]

    def wait(self, eng, ename, name, val=None):
        v = self.counts[name] if val is None else val
        if v <= 0:
            return
        key = (ename, name)
        if self.seen.get(key, -1) >= v:
            return
        eng.wait_ge(self.sems[name], v)
        self.seen[key] = v


def build_program():
    nc = bass.Bass("TRN2", target_bir_lowering=False)

    def din(name, shape, dt=F32):
        return nc.dram_tensor(name, shape, dt, kind="ExternalInput").ap()

    xT = din("xT", [D, T])
    x_ownT = din("x_ownT", [D, TH])
    winT = din("winT", [D, 2 * DH])
    binr = din("binr", [1, 2 * DH])
    convw = din("convw", [128, G4 * 4])
    convb = din("convb", [128, G4])
    alog = din("alog", [128, G4 * N])
    dparg = din("dparg", [128, G4])
    wxT = din("wxT", [DH, 32])
    bxc = din("bxc", [32, 1])
    woutT = din("woutT", [DH, D])
    boutg = din("boutg", [128, G8])
    n1wg = din("n1wg", [128, G8])
    n2wg = din("n2wg", [128, G8])
    qkvT = din("qkvT", [D, 3 * DH])
    owT = din("owT", [DH, D])
    gwT = din("gwT", [D, 8])
    upT = din("upT", [D, ED])
    dnT = din("dnT", [ED, D])
    triu1 = din("triu1", [128, 128])
    negeye = din("negeye", [128, 128])

    out_fm = nc.dram_tensor("out_fm", [D, TH], F32, kind="ExternalOutput").ap()

    def dint(name, shape, dt):
        return nc.dram_tensor(name, shape, dt)

    bc_in = [dint(f"bc_in{c}", [32, TC], F32) for c in range(NCH)]
    bc_out = [dint(f"bc_out{c}", [32, TC], F32) for c in range(NCH)]
    wo_in = dint("wo_in", [2, D, TH], BF16)
    wo_own = dint("wo_own", [1, D, TH], BF16)
    rms1_in = dint("rms1_in", [1, D, TH], BF16)
    rms1_out = dint("rms1_out", [2, D, TH], BF16)
    op_in = dint("op_in", [2, D, TH], BF16)
    op_own = dint("op_own", [1, D, TH], BF16)
    qQ = dint("qQ", [DH, T], BF16)
    qK = dint("qK", [DH, T], BF16)
    qV = dint("qV", [T, DH], BF16)
    xaf_d = dint("xaf_d", [D, TH], F32)
    up16 = dint("up16", [D, ED], BF16)
    dn16 = dint("dn16", [ED, D], BF16)

    with contextlib.ExitStack() as stack:
        E = stack.enter_context
        em = Emitter(nc, stack)
        for s in ["ld", "gld", "mm", "sc", "ve", "gp", "cc", "st",
                  "tr", "wld", "kvl", "bst", "wst", "rst", "xbl", "pqA", "pqB",
                  "mcx"]:
            em.sem(s)

        AR = E(nc.sbuf_tensor("AR", [128, ARN], F32))
        ARF = AR[:, :]
        ARB = ARF.bitcast(BF16)

        def fv(off, n):
            return ARF[:, off:off + n]

        def bv(off, n):
            return ARB[:, 2 * off:2 * off + n]

        s_convw = E(nc.sbuf_tensor("s_convw", [128, G4 * 4], F32))
        s_convb = E(nc.sbuf_tensor("s_convb", [128, G4], F32))
        s_A = E(nc.sbuf_tensor("s_A", [128, G4 * N], F32))
        s_dpar = E(nc.sbuf_tensor("s_dpar", [128, G4], F32))
        s_carry = E(nc.sbuf_tensor("s_carry", [128, G4 * N], F32))
        s_bxc = E(nc.sbuf_tensor("s_bxc", [32, 1], F32))
        s_boutg = E(nc.sbuf_tensor("s_boutg", [128, G8], F32))
        s_n1wg = E(nc.sbuf_tensor("s_n1wg", [128, G8], F32))
        s_n2wg = E(nc.sbuf_tensor("s_n2wg", [128, G8], F32))
        s_ones1 = E(nc.sbuf_tensor("s_ones1", [1, 128], F32))
        s_onesc = E(nc.sbuf_tensor("s_onesc", [128, 1], BF16))
        s_onesr = E(nc.sbuf_tensor("s_onesr", [1, TC], BF16))
        s_triu = E(nc.sbuf_tensor("s_triu", [128, 128], BF16))
        s_negi = E(nc.sbuf_tensor("s_negi", [128, 128], BF16))
        s_hsel = E(nc.sbuf_tensor("s_hsel", [128, 2], BF16))
        s_kmx = E(nc.sbuf_tensor("s_kmx", [128, G4], F32))
        s_fcol = E(nc.sbuf_tensor("s_fcol", [128, G4], F32))
        s_c1r = E(nc.sbuf_tensor("s_c1r", [128, 64], F32))
        s_g8 = E(nc.sbuf_tensor("s_g8", [128, 128], F32))
        s_gc = E(nc.sbuf_tensor("s_gc", [128, 4], F32))
        s_binr = E(nc.sbuf_tensor("s_binr", [1, 2 * DH], BF16))
        s_wx = E(nc.sbuf_tensor("s_wx", [128, G4 * 32], BF16))
        s_rows2 = E(nc.sbuf_tensor("s_rows2", [32, TC], F32))
        s_wout = E(nc.sbuf_tensor("s_wout", [128, G4 * 1024], BF16))
        s_stage = E(nc.sbuf_tensor("s_stage", [128, TC], F32))
        s_stage2 = E(nc.sbuf_tensor("s_stage2", [128, TC], F32))
        s_stage3 = E(nc.sbuf_tensor("s_stage3", [128, TC], F32))
        s_stage4 = E(nc.sbuf_tensor("s_stage4", [128, TC], F32))
        ROWS = E(nc.sbuf_tensor("rows", [1, 2048], F32))
        YB = E(nc.sbuf_tensor("yb", [128, 2 * G4 * TC], BF16))

        st2b = s_stage2[:, :].bitcast(BF16)
        st3b = s_stage3[:, :].bitcast(BF16)
        st4b = s_stage4[:, :].bitcast(BF16)

        pb = [E(nc.psum_tensor(f"pb{i}", [128, TC], F32)) for i in range(8)]

        blk = E(nc.Block())

        def on(engine_name, fn):
            getattr(blk, engine_name)(fn)

        rows_m = ROWS[0:1, 0:TC]
        rows_r = ROWS[0:1, TC:2 * TC]
        rows_mr = ROWS[0:1, 2 * TC:3 * TC]
        rows_t = ROWS[0:1, 3 * TC:4 * TC]
        rows_t2 = rows_mr

        def ph0_sync(e):
            for dst, src in [
                (s_convw[:, :], convw), (s_convb[:, :], convb),
                (s_A[:, :], alog), (s_dpar[:, :], dparg),
                (s_bxc[:, :], bxc),
                (s_boutg[:, :], boutg), (s_n1wg[:, :], n1wg),
                (s_n2wg[:, :], n2wg),
            ]:
                em.inc(e.dma_start(out=dst, in_=src), "ld", 16)
        on("sync", ph0_sync)
        ld0 = em.counts["ld"]

        def ph0_gp(e):
            for g in range(G8):
                em.inc(e.dma_start(out=bv(O_WIN, 16384)[:, g * 1024:(g + 1) * 1024],
                                   in_=winT[g * 128:(g + 1) * 128, :]), "gld", 16)
            for g in range(G4):
                em.inc(e.dma_start(out=s_wx[:, g * 32:(g + 1) * 32],
                                   in_=wxT[g * 128:(g + 1) * 128, :]), "gld", 16)
            for g in range(G4):
                em.inc(e.dma_start(out=s_wout[:, g * 1024:(g + 1) * 1024],
                                   in_=woutT[g * 128:(g + 1) * 128, :]), "gld", 16)
            em.inc(e.dma_start(out=s_binr[:, :], in_=binr), "gld", 16)
            em.inc(e.dma_start(out=s_triu[:, :], in_=triu1), "gld", 16)
            em.inc(e.dma_start(out=s_negi[:, :], in_=negeye), "gld", 16)
            e.memset(s_ones1[:, :], 1.0)
            for g in range(G4):
                e.memset(ARF[:, O_XS + g * 515:O_XS + g * 515 + 3], 0.0)
            e.memset(s_gc[:, 0:1], 0.0)
            e.memset(s_gc[:, 1:2], 1e-9)
            e.memset(s_gc[:, 2:3], 1.0)
            e.memset(s_onesc[:, :], 1.0)
            e.memset(s_onesr[:, :], 1.0)
            e.memset(s_carry[:, :], 0.0)
            e.memset(s_kmx[:, :], 0.0)
            e.memset(s_hsel[:, :], 0.0)
            e.memset(s_hsel[0:64, 0:1], 1.0)
            em.inc(e.memset(s_hsel[64:128, 1:2], 1.0), "gp", 1)
        on("gpsimd", ph0_gp)
        gld0 = em.counts["gld"]

        def ph0_scalar(e):
            em.wait(e, "scalar", "ld", ld0)
            e.activation(out=s_A[:, :], in_=s_A[:, :], func=AF.Exp)
            e.activation(out=s_A[:, :], in_=s_A[:, :], func=AF.Exp, scale=-1.0)
            e.mul(out=s_bxc[:, :], in_=s_bxc[:, :], mul=0.5)
            em.inc(e.mul(out=s_boutg[:, :], in_=s_boutg[:, :], mul=0.5), "sc", 1)
        on("scalar", ph0_scalar)
        sc0 = em.counts["sc"]

        xs_g = lambda g: fv(O_XS + g * 515, 515)
        xbf_g = lambda g: bv(O_XBF, 4096)[:, g * TC:(g + 1) * TC]
        ln1_g = lambda g: bv(O_LN1, 4096)[:, g * TC:(g + 1) * TC]
        H_all = fv(O_H, 8192)
        H_n = lambda n: fv(O_H + n * TC, TC)
        U_all = bv(O_U, 8192)
        U_n = lambda n: bv(O_U, 8192)[:, n * TC:(n + 1) * TC]
        REP = bv(O_REP, 16384)
        REPB = bv(O_REP, 8192)
        REPC = bv(O_REP + 4096, 8192)
        wvb = bv(O_XNA, 2048)
        wv_slot = lambda i: wvb[:, i * TC:(i + 1) * TC]

        def xcr(tc, g):
            r = tc % RINGM
            return bv(O_XCRR, RINGM * G4 * TC)[:, (r * G4 + g) * TC:(r * G4 + g + 1) * TC]

        def gtr(tc, g):
            r = tc % RINGM
            return bv(O_GTRR, RINGM * G4 * TC)[:, (r * G4 + g) * TC:(r * G4 + g + 1) * TC]

        def ybr(tc, g):
            r = tc % 2
            return YB[:, (r * G4 + g) * TC:(r * G4 + g + 1) * TC]

        statmm = {}
        scan_done = {}
        ygate_done = {}
        wout_mm = {}
        bcevac_sc = {}
        gate_evac_sc = {}
        wout_evac_sc = {}
        conv_done = {}
        bc_st = {}
        wo_ev_st = {}
        ar_cc = {}
        wo_st = []
        silu_prev = 0
        norm_prev_last = 0
        prev_mm_rep = 0
        prev_ve_r4a = 0
        back_ctx = {}

        def emit_scan(tc):
            tr_tree, sc_xc = back_ctx[tc]
            assert scan_done_static[tc] == em.counts["ve"] + 4, \
                (tc, scan_done_static[tc], em.counts["ve"])

            def f_scan(e, tc=tc, tr_tree=tr_tree, sc_xc=sc_xc):
                em.wait(e, "vector", "tr", tr_tree)
                em.wait(e, "vector", "sc", sc_xc)
                em.wait(e, "vector", "sc", gate_evac_sc[tc])
                if tc >= 2:
                    em.wait(e, "vector", "mm", wout_mm[tc - 2])
                for g in range(G4):
                    e.tensor_tensor(
                        out=U_all.rearrange("p (n t) -> p n t", n=N),
                        in0=xcr(tc, g).unsqueeze(1).to_broadcast([128, N, TC]),
                        in1=REPB.rearrange("p (n t) -> p n t", n=N),
                        op=OP.mult)
                    for n in range(N):
                        e.tensor_tensor_scan(
                            out=H_n(n),
                            data0=s_A[:, g * N + n:g * N + n + 1].to_broadcast([128, TC]),
                            data1=U_n(n),
                            initial=s_carry[:, g * N + n:g * N + n + 1],
                            op0=OP.mult, op1=OP.add)
                    e.tensor_copy(
                        s_carry[:, g * N:g * N + N - 1],
                        H_all.rearrange("p (n t) -> p n t", n=N)[:, 0:N - 1, TC - 1:TC])
                    e.tensor_copy(
                        s_carry[:, g * N + N - 1:g * N + N],
                        H_all[:, (N - 1) * TC + TC - 1:(N - 1) * TC + TC])
                    mc = e.tensor_tensor(out=H_all, in0=H_all, in1=REPC, op=OP.mult)
                    if g == G4 - 1:
                        em.inc(mc, "mcx", 1)
                    for half in (8, 4, 2):
                        hv = fv(O_H, 2 * half * TC).rearrange("p (a b t) -> p a b t",
                                                              a=half, b=2)
                        e.tensor_tensor(
                            out=fv(O_H, half * TC).rearrange("p (a t) -> p a t", a=half),
                            in0=hv[:, :, 0, :], in1=hv[:, :, 1, :], op=OP.add)
                    e.tensor_tensor(out=s_stage[:, :], in0=H_n(0), in1=H_n(1),
                                    op=OP.add)
                    e.scalar_tensor_tensor(
                        out=s_stage[:, :], in0=xcr(tc, g),
                        scalar=s_dpar[:, g:g + 1], in1=s_stage[:, :],
                        op0=OP.mult, op1=OP.add)
                    em.inc(e.tensor_tensor(out=ybr(tc, g), in0=s_stage[:, :],
                                           in1=gtr(tc, g), op=OP.mult), "ve", 1)
            on("vector", f_scan)
            scan_done[tc] = em.counts["ve"]
            ygate_done[tc] = em.counts["ve"]
            assert scan_done_static[tc] == em.counts["ve"]

        def emit_wout(tc):
            def f_wout_a(e, tc=tc):
                em.wait(e, "tensor", "ve", scan_done[tc])
                em.wait(e, "tensor", "sc", bcevac_sc[min(tc + 3, NCH - 1)])
                if tc >= 1:
                    em.wait(e, "tensor", "sc", wout_evac_sc[tc - 1])
                last = None
                for og in range(G4):
                    for kg in range(G4):
                        last = e.matmul(
                            out=pb[og][:, :],
                            lhsT=s_wout[:, kg * 1024 + og * 128: kg * 1024 + og * 128 + 128],
                            rhs=ybr(tc, kg), start=(kg == 0), stop=(kg == G4 - 1))
                em.inc(last, "mm", 1)
            on("tensor", f_wout_a)
            mm_wa = em.counts["mm"]

            def f_wout_ea(e, tc=tc, mm_wa=mm_wa):
                em.wait(e, "scalar", "mm", mm_wa)
                em.wait(e, "scalar", "wst", wo_ev_st.get((tc - 1, 1), 0))
                last = None
                for og in range(G4):
                    last = e.activation(out=wv_slot(og), in_=pb[og][:, :],
                                        func=AF.Identity, bias=s_boutg[:, og:og + 1])
                em.inc(last, "sc", 1)
            on("scalar", f_wout_ea)
            sc_wea = em.counts["sc"]

            def f_wout_da(e, tc=tc, sc_wea=sc_wea):
                em.wait(e, "sync", "sc", sc_wea)
                half = tc // 4
                col = (tc % 4) * TC
                for og in range(G4):
                    em.inc(e.dma_start(
                        out=wo_in.ap()[half:half + 1, og * 128:(og + 1) * 128, col:col + TC],
                        in_=wv_slot(og)), "wst", 16)
            on("sync", f_wout_da)
            wo_ev_st[(tc, 0)] = em.counts["wst"]

            def f_wout_b(e, tc=tc, sc_wea=sc_wea):
                em.wait(e, "tensor", "sc", sc_wea)
                last = None
                for og in range(4, G8):
                    for kg in range(G4):
                        last = e.matmul(
                            out=pb[og - 4][:, :],
                            lhsT=s_wout[:, kg * 1024 + og * 128: kg * 1024 + og * 128 + 128],
                            rhs=ybr(tc, kg), start=(kg == 0), stop=(kg == G4 - 1))
                em.inc(last, "mm", 1)
            on("tensor", f_wout_b)
            mm_wb = em.counts["mm"]
            wout_mm[tc] = mm_wb

            def f_wout_eb(e, tc=tc, mm_wb=mm_wb):
                em.wait(e, "scalar", "mm", mm_wb)
                em.wait(e, "scalar", "wst", wo_ev_st[(tc, 0)])
                last = None
                for og in range(4, G8):
                    last = e.activation(out=wv_slot(og - 4), in_=pb[og - 4][:, :],
                                        func=AF.Identity, bias=s_boutg[:, og:og + 1])
                em.inc(last, "sc", 1)
            on("scalar", f_wout_eb)
            sc_web = em.counts["sc"]
            wout_evac_sc[tc] = sc_web

            def f_wout_db(e, tc=tc, sc_web=sc_web):
                em.wait(e, "sync", "sc", sc_web)
                half = tc // 4
                col = (tc % 4) * TC
                for og in range(4, G8):
                    em.inc(e.dma_start(
                        out=wo_in.ap()[half:half + 1, og * 128:(og + 1) * 128, col:col + TC],
                        in_=wv_slot(og - 4)), "wst", 16)
            on("sync", f_wout_db)
            wo_ev_st[(tc, 1)] = em.counts["wst"]
            wo_st.append(em.counts["wst"])

        ve_base = em.counts["ve"]
        _order = []
        for _tc in range(NCH):
            _order.append(("f", _tc))
            if _tc >= 2:
                _order.append(("s", _tc - 2))
        _order += [("s", NCH - 2), ("s", NCH - 1)]
        scan_done_static = {}
        _acc = ve_base
        for kind, k in _order:
            _acc += 7 if kind == "f" else 4
            if kind == "s":
                scan_done_static[k] = _acc

        xbl_x = {}

        def emit_gpx(tcl, npl, smm):
            def f_gpx(e, tcl=tcl, npl=npl, smm=smm):
                if tcl >= 1:
                    em.wait(e, "gpsimd", "mm", smm)
                    em.wait(e, "gpsimd", "ve", npl)
                for g in range(G8):
                    em.inc(e.dma_start(out=xbf_g(g),
                                       in_=xT[g * 128:(g + 1) * 128, tcl * TC:(tcl + 1) * TC]),
                           "xbl", 16)
            on("gpsimd", f_gpx)
            xbl_x[tcl] = em.counts["xbl"]

        emit_gpx(0, 0, 0)

        for tc in range(NCH):
            gld_x = xbl_x[tc]

            def f_statx(e, tc=tc, gld_x=gld_x):
                em.wait(e, "tensor", "xbl", gld_x)
                if tc >= 1:
                    em.wait(e, "tensor", "sc", gate_evac_sc[tc - 1])
                last = None
                for g in range(G8):
                    last = e.matmul(out=pb[4][0:1, :], lhsT=s_onesc[:, :],
                                    rhs=xbf_g(g), start=(g == 0), stop=(g == G8 - 1))
                em.inc(last, "mm", 1)
            on("tensor", f_statx)

            sq_mm_prev = []
            for g in range(G8):
                slot = st4b[:, (g % 2) * TC:(g % 2) * TC + TC]

                def f_sqg(e, tc=tc, g=g, slot=slot, gld_x=gld_x):
                    em.wait(e, "scalar", "xbl", gld_x)
                    if g >= 2:
                        em.wait(e, "scalar", "mm", sq_mm_prev[g - 2])
                    elif tc >= 1:
                        em.wait(e, "scalar", "mm", statmm[tc - 1])
                    em.inc(e.activation(out=slot, in_=xbf_g(g), func=AF.Square),
                           "sc", 1)
                on("scalar", f_sqg)
                sc_g = em.counts["sc"]

                def f_sqmm(e, tc=tc, g=g, slot=slot, sc_g=sc_g):
                    em.wait(e, "tensor", "sc", sc_g)
                    em.inc(e.matmul(out=pb[5][0:1, :], lhsT=s_onesc[:, :], rhs=slot,
                                    start=(g == 0), stop=(g == G8 - 1)), "mm", 1)
                on("tensor", f_sqmm)
                sq_mm_prev.append(em.counts["mm"])
            statmm[tc] = em.counts["mm"]

            def f_rows1(e, tc=tc, pmm=prev_mm_rep, pve=prev_ve_r4a):
                em.wait(e, "scalar", "mm", statmm[tc])
                em.wait(e, "scalar", "mm", pmm)
                em.wait(e, "scalar", "ve", pve)
                e.activation(out=rows_m, in_=pb[4][0:1, :], func=AF.Copy, scale=1.0 / D)
                em.inc(e.activation(out=rows_t, in_=pb[5][0:1, :], func=AF.Copy,
                                    scale=1.0 / D), "sc", 1)
            on("scalar", f_rows1)
            sc_r1 = em.counts["sc"]

            def f_rows2(e, sc_r1=sc_r1):
                em.wait(e, "vector", "sc", sc_r1)
                e.tensor_mul(rows_t2, rows_m, rows_m)
                em.inc(e.tensor_sub(rows_t, rows_t, rows_t2), "ve", 1)
            on("vector", f_rows2)
            ve_r2 = em.counts["ve"]

            def f_rows3(e, ve_r2=ve_r2):
                em.wait(e, "scalar", "ve", ve_r2)
                em.inc(e.activation(out=rows_t, in_=rows_t, func=AF.Sqrt,
                                    bias=np.float32(1e-5)), "sc", 1)
            on("scalar", f_rows3)
            sc_r3 = em.counts["sc"]

            def f_rows4a(e, sc_r3=sc_r3):
                em.wait(e, "vector", "sc", sc_r3)
                em.inc(e.reciprocal(out=rows_r, in_=rows_t), "ve", 1)
            on("vector", f_rows4a)
            ve_r4a = em.counts["ve"]
            prev_ve_r4a = ve_r4a

            def f_rep(e, sc_r3=sc_r3, ve_r4a=ve_r4a, tc=tc):
                em.wait(e, "tensor", "sc", sc_r3)
                em.wait(e, "tensor", "ve", ve_r4a)
                if tc >= 1:
                    em.wait(e, "tensor", "ve", norm_prev_last)
                e.matmul(out=pb[6][:, :], lhsT=s_ones1[:, :], rhs=rows_r,
                         start=True, stop=True)
                em.inc(e.matmul(out=pb[7][:, :], lhsT=s_ones1[:, :], rhs=rows_m,
                                start=True, stop=True), "mm", 1)
            on("tensor", f_rep)
            mm_rep = em.counts["mm"]
            prev_mm_rep = mm_rep

            def f_nm(e, tc=tc, mm_rep=mm_rep):
                em.wait(e, "vector", "mm", mm_rep)
                last = None
                for g in range(G8):
                    e.tensor_sub(s_stage[:, :], xbf_g(g), pb[7][:, :])
                    last = e.tensor_mul(ln1_g(g), s_stage[:, :], pb[6][:, :])
                em.inc(last, "ve", 1)
            on("vector", f_nm)
            ve_norm = em.counts["ve"]
            norm_prev_last = ve_norm

            def f_win(e, tc=tc, ve_norm=ve_norm):
                em.wait(e, "tensor", "gld", gld0)
                em.wait(e, "tensor", "ve", ve_norm)
                if tc >= 1:
                    em.wait(e, "tensor", "sc", bcevac_sc[tc - 1])
                if tc >= 4:
                    em.wait(e, "tensor", "sc", wout_evac_sc[tc - 4])
                if tc >= RINGM:
                    em.wait(e, "tensor", "ve", ygate_done[tc - RINGM])
                last = None
                for og in range(G4):
                    e.matmul(out=pb[og][:, :],
                             lhsT=s_binr[0:1, og * 128:(og + 1) * 128],
                             rhs=s_onesr[:, :], start=True, stop=False)
                    for kg in range(G8):
                        last = e.matmul(
                            out=pb[og][:, :],
                            lhsT=bv(O_WIN, 16384)[:, kg * 1024 + og * 128: kg * 1024 + og * 128 + 128],
                            rhs=ln1_g(kg), start=False, stop=(kg == G8 - 1))
                em.inc(last, "mm", 1)
            on("tensor", f_win)
            mm_ssm = em.counts["mm"]

            def f_wing(e, tc=tc):
                last = None
                for og in range(G4):
                    bank = pb[4 + og]
                    e.matmul(out=bank[:, :],
                             lhsT=s_binr[0:1, 512 + og * 128: 512 + og * 128 + 128],
                             rhs=s_onesr[:, :], start=True, stop=False)
                    for kg in range(G8):
                        last = e.matmul(
                            out=bank[:, :],
                            lhsT=bv(O_WIN, 16384)[:, kg * 1024 + 512 + og * 128: kg * 1024 + 512 + og * 128 + 128],
                            rhs=ln1_g(kg), start=False, stop=(kg == G8 - 1))
                em.inc(last, "mm", 1)
            on("tensor", f_wing)
            mm_gate = em.counts["mm"]

            def f_evac1(e, tc=tc, mm_ssm=mm_ssm):
                em.wait(e, "scalar", "mm", mm_ssm)
                if tc >= 1:
                    em.wait(e, "scalar", "ve", conv_done[tc - 1])
                last = None
                for g in range(G4):
                    last = e.activation(out=xs_g(g)[:, 3:515], in_=pb[g][:, :],
                                        func=AF.Copy)
                em.inc(last, "sc", 1)
            on("scalar", f_evac1)
            sc_ev1 = em.counts["sc"]

            def f_evac2(e, tc=tc, mm_gate=mm_gate):
                em.wait(e, "scalar", "mm", mm_gate)
                if tc >= RINGM:
                    em.wait(e, "scalar", "ve", ygate_done[tc - RINGM])
                last = None
                for g in range(G4):
                    last = e.activation(out=gtr(tc, g), in_=pb[4 + g][:, :],
                                        func=AF.Silu)
                em.inc(last, "sc", 1)
            on("scalar", f_evac2)
            gate_evac_sc[tc] = em.counts["sc"]

            for gpair in range(2):
                def f_conv(e, tc=tc, gpair=gpair, sc_ev1=sc_ev1, silu_prev=silu_prev):
                    em.wait(e, "vector", "sc", sc_ev1)
                    em.wait(e, "vector", "sc", silu_prev)
                    for gi in range(2):
                        g = gpair * 2 + gi
                        dst = (s_stage2 if gi == 0 else s_stage3)[:, :]
                        e.tensor_scalar(out=dst, in0=xs_g(g)[:, 0:TC],
                                        scalar1=s_convw[:, g * 4:g * 4 + 1],
                                        scalar2=None, op0=OP.mult)
                        for j in (1, 2, 3):
                            e.scalar_tensor_tensor(
                                out=dst, in0=xs_g(g)[:, j:j + TC],
                                scalar=s_convw[:, g * 4 + j:g * 4 + j + 1],
                                in1=dst, op0=OP.mult, op1=OP.add)
                        em.inc(e.tensor_copy(xs_g(g)[:, 0:3], xs_g(g)[:, TC:TC + 3]),
                               "ve", 1)
                on("vector", f_conv)
                ve_cv = em.counts["ve"]

                def f_silu(e, tc=tc, gpair=gpair, ve_cv=ve_cv):
                    em.wait(e, "scalar", "ve", ve_cv)
                    if tc >= RINGM:
                        em.wait(e, "scalar", "ve", ygate_done[tc - RINGM])
                    for gi in range(2):
                        g = gpair * 2 + gi
                        src = (s_stage2 if gi == 0 else s_stage3)[:, :]
                        em.inc(e.activation(out=xcr(tc, g), in_=src, func=AF.Silu,
                                            bias=s_convb[:, g:g + 1]), "sc", 1)
                on("scalar", f_silu)
                silu_prev = em.counts["sc"]
            conv_done[tc] = em.counts["ve"]
            sc_xc = em.counts["sc"]

            def f_bc(e, tc=tc, sc_xc=sc_xc):
                em.wait(e, "tensor", "sc", sc_xc)
                if tc >= 1:
                    em.wait(e, "tensor", "sc", bcevac_sc[tc - 1])
                last = None
                for g in range(G4):
                    last = e.matmul(out=pb[0][0:32, :],
                                    lhsT=s_wx[:, g * 32:(g + 1) * 32],
                                    rhs=xcr(tc, g), start=(g == 0), stop=(g == G4 - 1))
                em.inc(last, "mm", 1)
            on("tensor", f_bc)
            mm_bc = em.counts["mm"]

            def f_bce(e, tc=tc, mm_bc=mm_bc):
                em.wait(e, "scalar", "mm", mm_bc)
                em.wait(e, "scalar", "bst", bc_st.get(tc - 1, 0))
                em.inc(e.activation(out=s_rows2[:, :], in_=pb[0][0:32, :],
                                    func=AF.Identity, bias=s_bxc[:, 0:1]), "sc", 1)
            on("scalar", f_bce)
            bcevac_sc[tc] = em.counts["sc"]

            def f_bcd(e, tc=tc):
                em.wait(e, "sync", "sc", bcevac_sc[tc])
                em.inc(e.dma_start(out=bc_in[tc].ap(), in_=s_rows2[:, :]), "bst", 16)
            on("sync", f_bcd)
            bc_st[tc] = em.counts["bst"]

            if tc + 1 < NCH:
                emit_gpx(tc + 1, ve_norm, statmm[tc])

            def f_ar(e, tc=tc):
                em.wait(e, "gpsimd", "bst", bc_st[tc])
                em.inc(e.collective_compute(
                    "AllReduce", OP.add, replica_groups=GROUPS,
                    ins=[bc_in[tc].ap().opt()], outs=[bc_out[tc].ap().opt()]), "cc", 1)
            on("gpsimd", f_ar)
            ar_cc[tc] = em.counts["cc"]

            def f_tree(e, tc=tc):
                em.wait(e, "gpsimd", "cc", ar_cc[tc])
                if tc >= 1:
                    em.wait(e, "gpsimd", "mcx", tc)
                em.inc(e.dma_start(
                    out=REP[0:1, :],
                    in_=bc_out[tc].ap().rearrange("a b -> (a b)")[None, :]), "tr", 16)
                nrep = 1
                while nrep < 32:
                    em.wait(e, "gpsimd", "tr")
                    em.inc(e.dma_start(out=REP[nrep:2 * nrep, :],
                                       in_=REP[0:nrep, :]), "tr", 16)
                    nrep *= 2
                em.wait(e, "gpsimd", "tr")
                for j in range(1, 4):
                    em.inc(e.dma_start(out=REP[32 * j:32 * (j + 1), :],
                                       in_=REP[0:32, :]), "tr", 16)
            on("gpsimd", f_tree)
            back_ctx[tc] = (em.counts["tr"], sc_xc)

            if tc >= 2:
                emit_scan(tc - 2)
            if tc >= 3:
                emit_wout(tc - 3)
        for tl in range(NCH, NCH + 3):
            if tl - 2 < NCH:
                emit_scan(tl - 2)
            emit_wout(tl - 3)

        def f_wors(e):
            em.wait(e, "gpsimd", "wst", max(wo_st))
            em.inc(e.collective_compute(
                "ReduceScatter", OP.add, replica_groups=GROUPS,
                ins=[wo_in.ap().opt()], outs=[wo_own.ap().opt()]), "cc", 1)
        on("gpsimd", f_wors)
        cc_wors = em.counts["cc"]

        def f_qkvw(e):
            em.wait(e, "gpsimd", "ve", scan_done[NCH - 1])
            em.wait(e, "gpsimd", "mm", wout_mm[NCH - 1])
            for g in range(G8):
                em.inc(e.dma_start(out=bv(O_QKV, 16384)[:, g * 1024:(g + 1) * 1024],
                                   in_=qkvT[g * 128:(g + 1) * 128, 0:1024]), "wld", 16)
            em.inc(e.dma_start(out=bv(O_GW, 128)[:, 0:64].rearrange("p (g c) -> p g c", g=8),
                               in_=gwT.rearrange("(g p) c -> p g c", p=128)), "wld", 16)
        on("gpsimd", f_qkvw)
        gld_qkvw = em.counts["wld"]



        xaf_g = lambda g, tco: fv(O_XAF + g * TH + tco * TC, TC)
        asm_ve = 0
        for tco in range(NCHO):
            for g in range(G8):
                def f_xal(e, tco=tco, g=g):
                    if asm_ve:
                        em.wait(e, "sync", "ve", asm_ve)
                    em.inc(e.dma_start(out=s_stage4[:, :],
                                       in_=x_ownT[g * 128:(g + 1) * 128, tco * TC:(tco + 1) * TC]),
                           "ld", 16)
                on("sync", f_xal)
                ld_a = em.counts["ld"]

                def f_xwo(e, tco=tco, g=g):
                    em.wait(e, "gpsimd", "cc", cc_wors)
                    if asm_ve:
                        em.wait(e, "gpsimd", "ve", asm_ve)
                    em.inc(e.dma_start(
                        out=st2b[:, 0:TC],
                        in_=wo_own.ap()[0:1, g * 128:(g + 1) * 128, tco * TC:(tco + 1) * TC]),
                        "gld", 16)
                on("gpsimd", f_xwo)
                gld_a = em.counts["gld"]

                def f_xadd(e, tco=tco, g=g, ld_a=ld_a, gld_a=gld_a):
                    em.wait(e, "vector", "ld", ld_a)
                    em.wait(e, "vector", "gld", gld_a)
                    em.inc(e.tensor_add(xaf_g(g, tco), s_stage4[:, :], st2b[:, 0:TC]),
                           "ve", 1)
                on("vector", f_xadd)
                asm_ve = em.counts["ve"]
        asm_done = em.counts["ve"]

        z_g = lambda g, tco: fv(O_Z + g * TH + tco * TC, TC)
        r1_st = []
        ln2_ve = 0
        prev_ve_5d = 0
        for tco in range(NCHO):
            sqmm = []
            for g in range(G8):
                xslot = st3b[:, (g % 2) * TC:(g % 2) * TC + TC]
                sslot = st4b[:, (g % 2) * TC:(g % 2) * TC + TC]

                pq = "pqA" if g % 2 == 0 else "pqB"

                def f_cast(e, tco=tco, g=g, xslot=xslot, pq=pq):
                    em.wait(e, "gpsimd", "ve", asm_done)
                    if g >= 2:
                        em.wait(e, "gpsimd", "mm", sqmm[g - 2][0])
                    em.inc(e.dma_start(out=xslot, in_=xaf_g(g, tco)), pq, 16)
                on("gpsimd", f_cast)
                gld_c = em.counts[pq]

                def f_sq5(e, tco=tco, g=g, xslot=xslot, sslot=sslot, gld_c=gld_c, pq=pq):
                    em.wait(e, "scalar", pq, gld_c)
                    if g >= 2:
                        em.wait(e, "scalar", "mm", sqmm[g - 2][1])
                    em.inc(e.activation(out=sslot, in_=xslot, func=AF.Square), "sc", 1)
                on("scalar", f_sq5)
                sc_q = em.counts["sc"]

                def f_mm5(e, tco=tco, g=g, xslot=xslot, sslot=sslot, sc_q=sc_q,
                          gld_c=gld_c, pq=pq):
                    em.wait(e, "tensor", pq, gld_c)
                    em.wait(e, "tensor", "sc", sc_q)
                    if tco == 0 and g == 0:
                        em.wait(e, "tensor", "wld", gld_qkvw)
                    m1 = e.matmul(out=pb[4][0:1, :], lhsT=s_onesc[:, :], rhs=xslot,
                                  start=(g == 0), stop=(g == G8 - 1))
                    em.inc(m1, "mm", 1)
                    m2 = e.matmul(out=pb[5][0:1, :], lhsT=s_onesc[:, :], rhs=sslot,
                                  start=(g == 0), stop=(g == G8 - 1))
                    em.inc(m2, "mm", 1)
                on("tensor", f_mm5)
                sqmm.append((em.counts["mm"] - 1, em.counts["mm"]))
            mm_s5 = em.counts["mm"]

            def f_rows5a(e, mm_s5=mm_s5, prev_ve_5d=prev_ve_5d):
                em.wait(e, "scalar", "mm", mm_s5)
                em.wait(e, "scalar", "gp", prev_ve_5d)
                e.activation(out=rows_m, in_=pb[4][0:1, :], func=AF.Copy, scale=1.0 / D)
                em.inc(e.activation(out=rows_t, in_=pb[5][0:1, :], func=AF.Copy,
                                    scale=1.0 / D), "sc", 1)
            on("scalar", f_rows5a)
            sc_5a = em.counts["sc"]

            def f_rows5b(e, sc_5a=sc_5a):
                em.wait(e, "gpsimd", "sc", sc_5a)
                e.tensor_mul(rows_t2, rows_m, rows_m)
                em.inc(e.tensor_sub(rows_t, rows_t, rows_t2), "gp", 1)
            on("gpsimd", f_rows5b)
            gp_5b = em.counts["gp"]

            def f_rows5c(e, gp_5b=gp_5b):
                em.wait(e, "scalar", "gp", gp_5b)
                em.inc(e.activation(out=rows_t, in_=rows_t, func=AF.Sqrt,
                                    bias=np.float32(1e-5)), "sc", 1)
            on("scalar", f_rows5c)
            sc_5c = em.counts["sc"]

            def f_rows5da(e, sc_5c=sc_5c):
                em.wait(e, "vector", "sc", sc_5c)
                em.inc(e.reciprocal(out=rows_r, in_=rows_t), "ve", 1)
            on("vector", f_rows5da)
            ve_5da = em.counts["ve"]

            def f_rows5d(e, ve_5da=ve_5da):
                em.wait(e, "gpsimd", "ve", ve_5da)
                em.inc(e.tensor_mul(rows_mr, rows_m, rows_r), "gp", 1)
            on("gpsimd", f_rows5d)
            ve_5d = em.counts["gp"]
            prev_ve_5d = ve_5d

            def f_rep5(e, ve_5d=ve_5d, tco=tco):
                em.wait(e, "tensor", "gp", ve_5d)
                if ln2_ve:
                    em.wait(e, "tensor", "ve", ln2_ve)
                e.matmul(out=pb[6][:, :], lhsT=s_ones1[:, :], rhs=rows_r,
                         start=True, stop=True)
                em.inc(e.matmul(out=pb[7][:, :], lhsT=s_ones1[:, :], rhs=rows_mr,
                                start=True, stop=True), "mm", 1)
            on("tensor", f_rep5)
            mm_rep5 = em.counts["mm"]

            def f_norm5(e, tco=tco, mm_rep5=mm_rep5):
                em.wait(e, "vector", "mm", mm_rep5)
                last = None
                for g in range(G8):
                    e.tensor_mul(s_stage[:, :], xaf_g(g, tco), pb[6][:, :])
                    last = e.tensor_sub(z_g(g, tco), s_stage[:, :], pb[7][:, :])
                em.inc(last, "ve", 1)
            on("vector", f_norm5)
            ln2_ve = em.counts["ve"]

            sq2mm = []
            for g in range(G8):
                sslot = st4b[:, (g % 2) * TC:(g % 2) * TC + TC]

                def f_sqr(e, tco=tco, g=g, sslot=sslot, ln2_ve=ln2_ve):
                    em.wait(e, "scalar", "ve", ln2_ve)
                    if g >= 2:
                        em.wait(e, "scalar", "mm", sq2mm[g - 2])
                    em.inc(e.activation(out=sslot, in_=z_g(g, tco), func=AF.Square),
                           "sc", 1)
                on("scalar", f_sqr)
                sc_q2 = em.counts["sc"]

                def f_mmr(e, g=g, sslot=sslot, sc_q2=sc_q2):
                    em.wait(e, "tensor", "sc", sc_q2)
                    em.inc(e.matmul(out=pb[4][0:1, :], lhsT=s_onesc[:, :], rhs=sslot,
                                    start=(g == 0), stop=(g == G8 - 1)), "mm", 1)
                on("tensor", f_mmr)
                sq2mm.append(em.counts["mm"])
            mm_r1 = em.counts["mm"]

            def f_rs1(e, mm_r1=mm_r1):
                em.wait(e, "scalar", "mm", mm_r1)
                em.inc(e.activation(out=rows_t, in_=pb[4][0:1, :], func=AF.Sqrt,
                                    scale=1.0 / D, bias=np.float32(1e-6)), "sc", 1)
            on("scalar", f_rs1)
            sc_rs1 = em.counts["sc"]

            def f_rs1v(e, sc_rs1=sc_rs1):
                em.wait(e, "vector", "sc", sc_rs1)
                em.inc(e.reciprocal(out=rows_r, in_=rows_t), "ve", 1)
            on("vector", f_rs1v)
            ve_rs1 = em.counts["ve"]

            def f_repr1(e, ve_rs1=ve_rs1):
                em.wait(e, "tensor", "ve", ve_rs1)
                em.inc(e.matmul(out=pb[6][:, :], lhsT=s_ones1[:, :], rhs=rows_r,
                                start=True, stop=True), "mm", 1)
            on("tensor", f_repr1)
            mm_repr1 = em.counts["mm"]

            for g in range(G8):
                slot = st2b[:, (g % 2) * TC:(g % 2) * TC + TC]

                def f_r1n(e, tco=tco, g=g, slot=slot, mm_repr1=mm_repr1):
                    em.wait(e, "vector", "mm", mm_repr1)
                    if len(r1_st) >= 2:
                        em.wait(e, "vector", "rst", r1_st[-1])
                    em.inc(e.scalar_tensor_tensor(
                        out=slot, in0=z_g(g, tco), scalar=s_n1wg[:, g:g + 1],
                        in1=pb[6][:, :], op0=OP.mult, op1=OP.mult), "ve", 1)
                on("vector", f_r1n)
                ve_r1n = em.counts["ve"]

                def f_r1d(e, tco=tco, g=g, slot=slot, ve_r1n=ve_r1n):
                    em.wait(e, "sync", "ve", ve_r1n)
                    em.inc(e.dma_start(
                        out=rms1_in.ap()[0:1, g * 128:(g + 1) * 128, tco * TC:(tco + 1) * TC],
                        in_=slot), "rst", 16)
                on("sync", f_r1d)
                r1_st.append(em.counts["rst"])
        rms1_last_st = em.counts["rst"]

        def f_pageout(e):
            em.wait(e, "sync", "ve", ln2_ve)
            for g in range(G8):
                em.inc(e.dma_start(out=xaf_d.ap()[g * 128:(g + 1) * 128, :],
                                   in_=fv(O_XAF + g * TH, TH)), "st", 16)
        on("sync", f_pageout)
        st_pageout = em.counts["st"]

        def f_qvw(e):
            em.wait(e, "gpsimd", "st", st_pageout)
            for g in range(G8):
                em.inc(e.dma_start(out=bv(O_QKVV, 4096)[:, g * 512:(g + 1) * 512],
                                   in_=qkvT[g * 128:(g + 1) * 128, 1024:1536]), "wld", 16)
        on("gpsimd", f_qvw)
        gld_qvw = em.counts["wld"]


        def f_ag(e):
            em.wait(e, "gpsimd", "rst", rms1_last_st)
            em.inc(e.collective_compute(
                "AllGather", OP.bypass, replica_groups=GROUPS,
                ins=[rms1_in.ap().opt()], outs=[rms1_out.ap().opt()]), "cc", 1)
        on("gpsimd", f_ag)
        cc_ag = em.counts["cc"]

        RMS1C = bv(O_RMS1C, 4096)
        qkv_ev_st = 0
        qst = {}
        for tc in range(NCH):
            def f_r1l(e, tc=tc):
                em.wait(e, "gpsimd", "cc", cc_ag)
                em.wait(e, "gpsimd", "st", st_pageout)
                if tc >= 1:
                    em.wait(e, "gpsimd", "mm", vmm_last)
                slot = tc // 4
                col = (tc % 4) * TC
                for g in range(G8):
                    em.inc(e.dma_start(
                        out=RMS1C[:, g * TC:(g + 1) * TC],
                        in_=rms1_out.ap()[slot:slot + 1, g * 128:(g + 1) * 128, col:col + TC]),
                        "gld", 16)
            on("gpsimd", f_r1l)
            gld_r1 = em.counts["gld"]

            def f_qk(e, tc=tc, gld_r1=gld_r1):
                em.wait(e, "tensor", "gld", gld_r1)
                em.wait(e, "tensor", "wld", gld_qkvw)
                if tc >= 1:
                    em.wait(e, "tensor", "sc", sc_kev)
                    em.wait(e, "tensor", "sc", sc_vev)
                last = None
                for og in range(G4):
                    for kg in range(G8):
                        last = e.matmul(
                            out=pb[og][:, :],
                            lhsT=bv(O_QKV, 16384)[:, kg * 1024 + og * 128: kg * 1024 + og * 128 + 128],
                            rhs=RMS1C[:, kg * TC:(kg + 1) * TC],
                            start=(kg == 0), stop=(kg == G8 - 1))
                em.inc(last, "mm", 1)
                last = None
                for og in range(G4):
                    for kg in range(G8):
                        last = e.matmul(
                            out=pb[4 + og][:, :],
                            lhsT=bv(O_QKV, 16384)[:, kg * 1024 + 512 + og * 128: kg * 1024 + 512 + og * 128 + 128],
                            rhs=RMS1C[:, kg * TC:(kg + 1) * TC],
                            start=(kg == 0), stop=(kg == G8 - 1))
                em.inc(last, "mm", 1)
            sc_kev = qst.get("sc_kev", 0)
            sc_vev = qst.get("sc_vev", 0)
            on("tensor", f_qk)
            mm_q = em.counts["mm"] - 1
            mm_k = em.counts["mm"]

            def f_qev(e, tc=tc, mm_q=mm_q):
                em.wait(e, "scalar", "mm", mm_q)
                em.wait(e, "scalar", "st", qst.get("v_st", 0))
                last = None
                for og in range(G4):
                    dst = (st2b if og % 2 == 0 else st3b)[:, (og // 2) * TC:(og // 2) * TC + TC]
                    last = e.activation(out=dst, in_=pb[og][:, :], func=AF.Copy)
                em.inc(last, "sc", 1)
            on("scalar", f_qev)
            sc_qev = em.counts["sc"]

            q2mms = []
            for og in range(G4):
                def f_sqog(e, tc=tc, og=og):
                    if og >= 2:
                        em.wait(e, "scalar", "mm", q2mms[og - 2])
                    src_ = (st2b if og % 2 == 0 else st3b)[:, (og // 2) * TC:(og // 2) * TC + TC]
                    em.inc(e.activation(out=st4b[:, (og % 2) * TC:(og % 2) * TC + TC],
                                        in_=src_, func=AF.Square), "sc", 1)
                on("scalar", f_sqog)
                sc_og = em.counts["sc"]

                def f_q2og(e, tc=tc, og=og, sc_og=sc_og):
                    em.wait(e, "tensor", "sc", sc_og)
                    em.inc(e.matmul(out=pb[og][0:2, :], lhsT=s_hsel[:, :],
                                    rhs=st4b[:, (og % 2) * TC:(og % 2) * TC + TC],
                                    start=True, stop=True), "mm", 1)
                on("tensor", f_q2og)
                q2mms.append(em.counts["mm"])
            mm_q2 = em.counts["mm"]

            def f_q2ev(e, tc=tc, mm_q2=mm_q2):
                em.wait(e, "vector", "mm", mm_q2)
                last = None
                for og in range(G4):
                    last = e.tensor_reduce(out=s_g8[0:2, og * 8 + tc: og * 8 + tc + 1],
                                           in_=pb[og][0:2, :], axis=AX.X, op=OP.max)
                em.inc(last, "ve", 1)
            on("vector", f_q2ev)
            sc_q2ev = em.counts["ve"]

            def f_qdma(e, tc=tc, sc_qev=sc_qev):
                em.wait(e, "sync", "sc", sc_qev)
                for og in range(G4):
                    src = (st2b if og % 2 == 0 else st3b)[:, (og // 2) * TC:(og // 2) * TC + TC]
                    em.inc(e.dma_start(
                        out=qQ.ap()[og * 128:(og + 1) * 128, tc * TC:(tc + 1) * TC],
                        in_=src), "st", 16)
            on("sync", f_qdma)
            qst["q_st"] = em.counts["st"]

            def f_kev(e, tc=tc, mm_k=mm_k):
                em.wait(e, "scalar", "mm", mm_k)
                em.wait(e, "scalar", "st", qst["q_st"])
                last = None
                for og in range(G4):
                    dst = (st2b if og % 2 == 0 else st3b)[:, (og // 2) * TC:(og // 2) * TC + TC]
                    last = e.activation(out=dst, in_=pb[4 + og][:, :], func=AF.Copy)
                em.inc(last, "sc", 1)
            on("scalar", f_kev)
            qst["sc_kev"] = sc_kev2 = em.counts["sc"]

            def f_kmax(e, tc=tc, sc_kev2=sc_kev2):
                em.wait(e, "vector", "sc", sc_kev2)
                for og in range(G4):
                    src = (st2b if og % 2 == 0 else st3b)[:, (og // 2) * TC:(og // 2) * TC + TC]
                    e.tensor_reduce(out=s_stage[:, og:og + 1], in_=src, axis=AX.X,
                                    op=OP.max, apply_absolute_value=True)
                last = None
                for og in range(G4):
                    last = e.tensor_max(s_kmx[:, og:og + 1], s_kmx[:, og:og + 1],
                                        s_stage[:, og:og + 1])
                em.inc(last, "ve", 1)
            on("vector", f_kmax)
            ve_km = em.counts["ve"]

            def f_kdma(e, tc=tc, ve_km=ve_km):
                em.wait(e, "sync", "ve", ve_km)
                for og in range(G4):
                    src = (st2b if og % 2 == 0 else st3b)[:, (og // 2) * TC:(og // 2) * TC + TC]
                    em.inc(e.dma_start(
                        out=qK.ap()[og * 128:(og + 1) * 128, tc * TC:(tc + 1) * TC],
                        in_=src), "st", 16)
            on("sync", f_kdma)
            qst["k_st"] = em.counts["st"]

            def f_v(e, tc=tc, sc_q2ev=sc_q2ev):
                em.wait(e, "tensor", "ve", sc_q2ev)
                em.wait(e, "tensor", "wld", gld_qvw)
                for ts in range(4):
                    last = None
                    for kg in range(G8):
                        last = e.matmul(
                            out=pb[ts][:, :],
                            lhsT=RMS1C[:, kg * TC + ts * 128: kg * TC + ts * 128 + 128],
                            rhs=bv(O_QKVV, 4096)[:, kg * 512:(kg + 1) * 512],
                            start=(kg == 0), stop=(kg == G8 - 1))
                    em.inc(last, "mm", 1)
            on("tensor", f_v)
            vmm_last = em.counts["mm"]

            def f_vev(e, tc=tc, vmm_last=vmm_last):
                em.wait(e, "scalar", "mm", vmm_last)
                em.wait(e, "scalar", "st", qst["k_st"])
                last = None
                for ts in range(4):
                    dst = (st2b if ts % 2 == 0 else st3b)[:, (ts // 2) * TC:(ts // 2) * TC + TC]
                    last = e.activation(out=dst, in_=pb[ts][:, :], func=AF.Copy)
                em.inc(last, "sc", 1)
            on("scalar", f_vev)
            qst["sc_vev"] = sc_vev2 = em.counts["sc"]

            def f_vdma(e, tc=tc, sc_vev2=sc_vev2):
                em.wait(e, "sync", "sc", sc_vev2)
                for ts in range(4):
                    src = (st2b if ts % 2 == 0 else st3b)[:, (ts // 2) * TC:(ts // 2) * TC + TC]
                    em.inc(e.dma_start(
                        out=qV.ap()[tc * TC + ts * 128: tc * TC + (ts + 1) * 128, :],
                        in_=src), "st", 16)
            on("sync", f_vdma)
            qst["v_st"] = em.counts["st"]
        qkv_done_st = em.counts["st"]
        qkv_done_ve = em.counts["ve"]

        def f_c1a(e):
            em.wait(e, "gpsimd", "ve", qkv_done_ve)
            em.inc(e.dma_start(out=ROWS[0:1, 0:512], in_=s_kmx[:, :]), "tr", 16)
            for hh in range(2):
                em.inc(e.dma_start(
                    out=ROWS[0:1, 520:584].rearrange("a (og hh2 tc) -> a og hh2 tc", og=4, hh2=2)[:, :, hh, :],
                    in_=s_g8[hh:hh + 1, 0:32], allow_slow_non_contiguous=True), "tr", 16)
        on("gpsimd", f_c1a)
        gld_c1a = em.counts["tr"]

        def f_c1b(e, gld_c1a=gld_c1a):
            em.wait(e, "vector", "tr", gld_c1a)
            em.inc(e.tensor_reduce(
                out=ROWS[0:1, 512:520].rearrange("a (kg hh) -> a kg hh", kg=4),
                in_=ROWS[0:1, 0:512].rearrange("a (hh i kg) -> a kg hh i", hh=2, i=64, kg=4),
                axis=AX.X, op=OP.max), "ve", 1)
        on("vector", f_c1b)
        ve_c1b = em.counts["ve"]

        def f_c1d(e, ve_c1b=ve_c1b):
            em.wait(e, "scalar", "ve", ve_c1b)
            e.mul(out=ROWS[0:1, 512:520], in_=ROWS[0:1, 512:520], mul=-0.125)
            em.inc(e.activation(out=ROWS[0:1, 520:584], in_=ROWS[0:1, 520:584],
                                func=AF.Sqrt), "sc", 1)
        on("scalar", f_c1d)
        sc_c1d = em.counts["sc"]

        def f_c1e(e, sc_c1d=sc_c1d):
            em.wait(e, "gpsimd", "sc", sc_c1d)
            e.tensor_tensor(
                out=ROWS[0:1, 520:584].rearrange("a (h tc) -> a h tc", h=8),
                in0=ROWS[0:1, 520:584].rearrange("a (h tc) -> a h tc", h=8),
                in1=ROWS[0:1, 512:520].unsqueeze(2).to_broadcast([1, 8, 8]),
                op=OP.mult)
            em.inc(e.tensor_scalar(out=ROWS[0:1, 520:584], in0=ROWS[0:1, 520:584],
                                   scalar1=75.0, scalar2=None, op0=OP.add), "gp", 1)
        on("gpsimd", f_c1e)
        ve_c1e = em.counts["gp"]

        def f_c1f(e, ve_c1e=ve_c1e):
            em.wait(e, "gpsimd", "gp", ve_c1e)
            em.inc(e.dma_start(out=s_c1r[0:1, :], in_=ROWS[0:1, 520:584]), "tr", 16)
            nrep = 1
            while nrep < 128:
                em.wait(e, "gpsimd", "tr")
                em.inc(e.dma_start(out=s_c1r[nrep:2 * nrep, :],
                                   in_=s_c1r[0:nrep, :]), "tr", 16)
                nrep *= 2
        on("gpsimd", f_c1f)
        gld_c1f = em.counts["tr"]

        def f_oww(e):
            em.wait(e, "gpsimd", "mm", vmm_last)
            for g in range(G4):
                em.inc(e.dma_start(out=bv(O_OW, 8192)[:, g * 1024:(g + 1) * 1024],
                                   in_=owT[g * 128:(g + 1) * 128, :]), "wld", 16)
        on("gpsimd", f_oww)
        gld_ow = em.counts["wld"]

        pc_tiles = []
        for r in range(8):
            for c in range(4):
                pc_tiles.append((upT[r * 128:(r + 1) * 128, c * 1024:(c + 1) * 1024],
                                 up16.ap()[r * 128:(r + 1) * 128, c * 1024:(c + 1) * 1024]))
        for r in range(32):
            pc_tiles.append((dnT[r * 128:(r + 1) * 128, :],
                             dn16.ap()[r * 128:(r + 1) * 128, :]))
        pc_state = {"i": 0, "store": {}}
        pcslot = [bv(36864, 1024), bv(37376, 1024)]

        def emit_precast(n):
            def f_pc(e, n=n):
                for _ in range(n):
                    i = pc_state["i"]
                    if i >= len(pc_tiles):
                        break
                    src, dst = pc_tiles[i]
                    pq = "pqA" if i % 2 == 0 else "pqB"
                    if i >= 2:
                        em.wait(e, "gpsimd", "rst", pc_state["store"][i - 2])
                    em.inc(e.dma_start(out=pcslot[i % 2], in_=src), pq, 16)
                    em.wait(e, "gpsimd", pq, em.counts[pq])
                    em.inc(e.dma_start(out=dst, in_=pcslot[i % 2]), "rst", 16)
                    pc_state["store"][i] = em.counts["rst"]
                    pc_state["i"] += 1
            on("gpsimd", f_pc)

        KP = bv(O_KP, 4096)
        VT = [bv(O_VT0, 2080), bv(O_VT1, 2080)]
        QT = bv(O_QT, 512)
        PT = [bv(O_P0, 1024), bv(O_P1, 1024)]
        AVv = bv(O_AV, 1024)
        AVN = bv(O_AVN, 512)
        RATT = bv(O_RATT, 16384)

        att_gp = {}
        qt_mm_last = 0
        att_prev_ve = 0
        att_prev_mm = 0
        att_prev_pv = 0
        att_prev_exp = 0
        att_prev_sc = 0
        for hp in range(G4):
            def f_kvload(e, hp=hp):
                em.wait(e, "gpsimd", "tr", gld_c1f)
                em.wait(e, "gpsimd", "st", qkv_done_st)
                em.wait(e, "gpsimd", "mm", att_prev_mm if hp >= 1 else vmm_last)
                em.inc(e.dma_start(out=KP[:, :], in_=qK.ap()[hp * 128:(hp + 1) * 128, :]),
                       "kvl", 16)
                for hh in range(2):
                    h0 = 64 * (2 * hp + hh)
                    em.inc(e.memset(VT[hh][:, :], 1.0), "kvl", 1)
                    for q in range(4):
                        em.inc(e.dma_start(
                            out=VT[hh].rearrange("p (kt j) -> p kt j", j=65)[:, q * 8:(q + 1) * 8, 0:64],
                            in_=qV.ap().rearrange("(kt p) d -> p kt d", p=128)[:, q * 8:(q + 1) * 8, h0:h0 + 64]),
                            "kvl", 16)
            on("gpsimd", f_kvload)
            gld_kv = em.counts["kvl"]

            for qc in range(NCH):
                def f_ql(e, hp=hp, qc=qc, qt_prev=None):
                    em.wait(e, "sync", "mm", qt_mm_last)
                    em.inc(e.dma_start(out=QT[:, :],
                                       in_=qQ.ap()[hp * 128:(hp + 1) * 128, qc * TC:(qc + 1) * TC]),
                           "ld", 16)
                on("sync", f_ql)
                ld_q = em.counts["ld"]

                ktmax = min(NKT - 1, 4 * qc + 3)
                nkts = ktmax + 1
                prev_pv = att_prev_pv
                prev_exp = att_prev_exp
                mm0 = em.counts["mm"]
                sc0 = em.counts["sc"]
                ve0 = em.counts["ve"]
                smm_p = {kt: mm0 + 2 * kt + 1 for kt in range(nkts)}
                pv_p = {kt: mm0 + 2 * kt + 2 for kt in range(nkts)}
                mm_rc = mm0 + 2 * nkts + 1
                exp_p = {kt: sc0 + kt + 1 for kt in range(nkts)}
                sc_av = sc0 + nkts + 1
                pz_kts = [kt for kt in range(nkts) if kt * 128 - qc * TC > 0]
                msets = {kt: ve0 + i + 1 for i, kt in enumerate(pz_kts)}
                ve_rc = ve0 + len(pz_kts) + 1
                ve_at = ve_rc + 1

                def f_qc_t(e, hp=hp, qc=qc, ktmax=ktmax, ld_q=ld_q, gld_kv=gld_kv,
                           prev_exp=prev_exp, exp_p=exp_p, msets=msets, ve_rc=ve_rc):
                    em.wait(e, "tensor", "ld", ld_q)
                    em.wait(e, "tensor", "kvl", gld_kv)
                    for kt in range(ktmax + 1):
                        j0 = max(0, kt * 128 - qc * TC)
                        bank0 = pb[0] if kt % 2 == 0 else pb[6]
                        bank1 = pb[1] if kt % 2 == 0 else pb[7]
                        boundary = (kt >= 4 * qc)
                        em.wait(e, "tensor", "sc", exp_p.get(kt - 2, prev_exp))
                        e.matmul(out=bank0[:, j0:TC],
                                 lhsT=KP[0:64, kt * 128:(kt + 1) * 128],
                                 rhs=QT[0:64, j0:TC],
                                 start=True, stop=not boundary,
                                 tile_position=(0, 0), skip_group_check=True)
                        m1 = e.matmul(out=bank1[:, j0:TC],
                                      lhsT=KP[64:128, kt * 128:(kt + 1) * 128],
                                      rhs=QT[64:128, j0:TC],
                                      start=True, stop=not boundary,
                                      tile_position=(64, 0), skip_group_check=True)
                        last = m1
                        if boundary:
                            e.matmul(out=bank0[:, j0:j0 + 128], lhsT=s_triu[:, :],
                                     rhs=s_negi[:, :], start=False, stop=True,
                                     skip_group_check=True)
                            last = e.matmul(out=bank1[:, j0:j0 + 128],
                                            lhsT=s_triu[:, :], rhs=s_negi[:, :],
                                            start=False, stop=True,
                                            skip_group_check=True)
                        em.inc(last, "mm", 1)
                        em.wait(e, "tensor", "sc", exp_p[kt])
                        if kt in msets:
                            em.wait(e, "tensor", "ve", msets[kt])
                        if kt == 0 and (qc >= 1 or hp >= 1):
                            em.wait(e, "tensor", "ve", att_prev_ve)
                        e.matmul(out=pb[2][0:65, :],
                                 lhsT=VT[0][:, kt * 65:(kt + 1) * 65],
                                 rhs=PT[0][:, (kt % 2) * TC:(kt % 2) * TC + TC],
                                 start=(kt == 0), stop=(kt == ktmax),
                                 skip_group_check=True)
                        em.inc(e.matmul(out=pb[3][0:65, :],
                                        lhsT=VT[1][:, kt * 65:(kt + 1) * 65],
                                        rhs=PT[1][:, (kt % 2) * TC:(kt % 2) * TC + TC],
                                        start=(kt == 0), stop=(kt == ktmax),
                                        skip_group_check=True), "mm", 1)
                    em.wait(e, "tensor", "ve", ve_rc)
                    e.matmul(out=pb[4][:, :], lhsT=s_ones1[:, :],
                             rhs=ROWS[0:1, 1024:1536], start=True, stop=True)
                    em.inc(e.matmul(out=pb[5][:, :], lhsT=s_ones1[:, :],
                                    rhs=ROWS[0:1, 1536:2048], start=True, stop=True),
                           "mm", 1)
                on("tensor", f_qc_t)
                assert em.counts["mm"] == mm_rc, (em.counts["mm"], mm_rc)

                def f_qc_s(e, hp=hp, qc=qc, ktmax=ktmax, smm_p=smm_p, pv_p=pv_p,
                           prev_pv=prev_pv):
                    c0 = (2 * hp) * 8 + qc
                    c1c = (2 * hp + 1) * 8 + qc
                    for kt in range(ktmax + 1):
                        j0 = max(0, kt * 128 - qc * TC)
                        bank0 = pb[0] if kt % 2 == 0 else pb[6]
                        bank1 = pb[1] if kt % 2 == 0 else pb[7]
                        em.wait(e, "scalar", "mm", smm_p[kt])
                        em.wait(e, "scalar", "mm", pv_p.get(kt - 2, prev_pv))
                        e.activation(out=PT[0][:, (kt % 2) * TC + j0:(kt % 2) * TC + TC],
                                     in_=bank0[:, j0:TC], func=AF.Exp, scale=0.125,
                                     bias=s_c1r[:, c0:c0 + 1])
                        em.inc(e.activation(out=PT[1][:, (kt % 2) * TC + j0:(kt % 2) * TC + TC],
                                            in_=bank1[:, j0:TC], func=AF.Exp,
                                            scale=0.125, bias=s_c1r[:, c1c:c1c + 1]),
                               "sc", 1)
                    em.wait(e, "scalar", "mm", pv_p[ktmax])
                    e.activation(out=s_stage[0:64, :], in_=pb[2][0:64, :], func=AF.Copy)
                    em.inc(e.activation(out=s_stage2[0:64, :], in_=pb[3][0:64, :],
                                        func=AF.Copy), "sc", 1)
                on("scalar", f_qc_s)
                assert em.counts["sc"] == sc_av, (em.counts["sc"], sc_av)

                def f_qc_v(e, hp=hp, qc=qc, pz_kts=pz_kts, pv_p=pv_p, prev_pv=prev_pv,
                           sc_av=sc_av, mm_rc=mm_rc):
                    for kt in pz_kts:
                        j0 = kt * 128 - qc * TC
                        em.wait(e, "vector", "mm", pv_p.get(kt - 2, prev_pv))
                        e.memset(PT[0][:, (kt % 2) * TC:(kt % 2) * TC + j0], 0.0)
                        em.inc(e.memset(PT[1][:, (kt % 2) * TC:(kt % 2) * TC + j0], 0.0),
                               "ve", 1)
                    em.wait(e, "vector", "mm", pv_p[max(pv_p)])
                    e.reciprocal(out=ROWS[0:1, 1024:1536], in_=pb[2][64:65, :])
                    em.inc(e.reciprocal(out=ROWS[0:1, 1536:2048], in_=pb[3][64:65, :]),
                           "ve", 1)
                    em.wait(e, "vector", "sc", sc_av)
                    em.wait(e, "vector", "mm", mm_rc)
                    if att_gp.get("avn_gld"):
                        em.wait(e, "vector", "gld", att_gp["avn_gld"])
                    e.tensor_tensor(out=RATT[0:64, hp * T + qc * TC: hp * T + (qc + 1) * TC],
                                    in0=s_stage[0:64, :], in1=pb[4][0:64, :], op=OP.mult)
                    em.inc(e.tensor_tensor(out=AVN[0:64, :], in0=s_stage2[0:64, :],
                                           in1=pb[5][0:64, :], op=OP.mult), "ve", 1)
                on("vector", f_qc_v)
                assert em.counts["ve"] == ve_at, (em.counts["ve"], ve_at)

                qt_mm_last = pv_p[ktmax]
                att_prev_pv = pv_p[ktmax]
                att_prev_exp = exp_p[ktmax]
                att_prev_sc = sc_av
                att_prev_ve = ve_at

                def f_avmv(e, hp=hp, qc=qc):
                    em.wait(e, "gpsimd", "ve", att_prev_ve)
                    em.inc(e.dma_start(
                        out=RATT[64:128, hp * T + qc * TC: hp * T + (qc + 1) * TC],
                        in_=AVN[0:64, :]), "gld", 16)
                on("gpsimd", f_avmv)
                att_gp["avn_gld"] = em.counts["gld"]
            emit_precast(16)
            att_prev_mm = em.counts["mm"]
        attn_done_gld = att_gp["avn_gld"]
        attn_done_ve = att_prev_ve
        assert pc_state["i"] == len(pc_tiles)
        precast_done = em.counts["rst"]

        osts = []
        o_ev_st = 0
        for tc in range(NCH):
            def f_omm_a(e, tc=tc):
                em.wait(e, "tensor", "wld", gld_ow)
                em.wait(e, "tensor", "gld", attn_done_gld)
                em.wait(e, "tensor", "ve", attn_done_ve)
                if tc >= 1:
                    em.wait(e, "tensor", "sc", o_sc[(tc - 1, 1)])
                last = None
                for og in range(G4):
                    for kg in range(G4):
                        last = e.matmul(
                            out=pb[og][:, :],
                            lhsT=bv(O_OW, 8192)[:, kg * 1024 + og * 128: kg * 1024 + og * 128 + 128],
                            rhs=RATT[:, kg * T + tc * TC: kg * T + (tc + 1) * TC],
                            start=(kg == 0), stop=(kg == G4 - 1))
                em.inc(last, "mm", 1)
            o_sc = {} if tc == 0 else o_sc
            on("tensor", f_omm_a)
            mm_oa = em.counts["mm"]

            def f_oev_a(e, tc=tc, mm_oa=mm_oa):
                em.wait(e, "scalar", "mm", mm_oa)
                em.wait(e, "scalar", "wst", o_st.get((tc - 1, 1), 0))
                last = None
                for og in range(G4):
                    dst = (st2b if og % 2 == 0 else st3b)[:, (og // 2) * TC:(og // 2) * TC + TC]
                    last = e.activation(out=dst, in_=pb[og][:, :], func=AF.Copy)
                em.inc(last, "sc", 1)
            o_st = {} if tc == 0 else o_st
            on("scalar", f_oev_a)
            o_sc[(tc, 0)] = em.counts["sc"]

            def f_odma_a(e, tc=tc):
                em.wait(e, "sync", "sc", o_sc[(tc, 0)])
                half = tc // 4
                col = (tc % 4) * TC
                for og in range(G4):
                    src = (st2b if og % 2 == 0 else st3b)[:, (og // 2) * TC:(og // 2) * TC + TC]
                    em.inc(e.dma_start(
                        out=op_in.ap()[half:half + 1, og * 128:(og + 1) * 128, col:col + TC],
                        in_=src), "wst", 16)
            on("sync", f_odma_a)
            o_st[(tc, 0)] = em.counts["wst"]

            def f_omm_b(e, tc=tc):
                em.wait(e, "tensor", "sc", o_sc[(tc, 0)])
                last = None
                for og in range(4, G8):
                    for kg in range(G4):
                        last = e.matmul(
                            out=pb[og - 4][:, :],
                            lhsT=bv(O_OW, 8192)[:, kg * 1024 + og * 128: kg * 1024 + og * 128 + 128],
                            rhs=RATT[:, kg * T + tc * TC: kg * T + (tc + 1) * TC],
                            start=(kg == 0), stop=(kg == G4 - 1))
                em.inc(last, "mm", 1)
            on("tensor", f_omm_b)
            mm_ob = em.counts["mm"]

            def f_oev_b(e, tc=tc, mm_ob=mm_ob):
                em.wait(e, "scalar", "mm", mm_ob)
                em.wait(e, "scalar", "wst", o_st[(tc, 0)])
                last = None
                for og in range(4, G8):
                    dst = (st2b if og % 2 == 0 else st3b)[:, ((og - 4) // 2) * TC:((og - 4) // 2) * TC + TC]
                    last = e.activation(out=dst, in_=pb[og - 4][:, :], func=AF.Copy)
                em.inc(last, "sc", 1)
            on("scalar", f_oev_b)
            o_sc[(tc, 1)] = em.counts["sc"]

            def f_odma_b(e, tc=tc):
                em.wait(e, "sync", "sc", o_sc[(tc, 1)])
                half = tc // 4
                col = (tc % 4) * TC
                for og in range(4, G8):
                    src = (st2b if og % 2 == 0 else st3b)[:, ((og - 4) // 2) * TC:((og - 4) // 2) * TC + TC]
                    em.inc(e.dma_start(
                        out=op_in.ap()[half:half + 1, og * 128:(og + 1) * 128, col:col + TC],
                        in_=src), "wst", 16)
            on("sync", f_odma_b)
            o_st[(tc, 1)] = em.counts["wst"]
            osts.append(em.counts["wst"])

        def f_ors(e):
            em.wait(e, "gpsimd", "wst", max(osts))
            em.inc(e.collective_compute(
                "ReduceScatter", OP.add, replica_groups=GROUPS,
                ins=[op_in.ap().opt()], outs=[op_own.ap().opt()]), "cc", 1)
        on("gpsimd", f_ors)
        cc_ors = em.counts["cc"]

        zup_ve = 0
        for tco in range(NCHO):
            for g in range(G8):
                def f_ol(e, tco=tco, g=g):
                    em.wait(e, "gpsimd", "cc", cc_ors)
                    if zup_ve:
                        em.wait(e, "gpsimd", "ve", zup_ve)
                    em.inc(e.dma_start(
                        out=st2b[:, 0:TC],
                        in_=op_own.ap()[0:1, g * 128:(g + 1) * 128, tco * TC:(tco + 1) * TC]),
                        "gld", 16)
                on("gpsimd", f_ol)
                gld_o = em.counts["gld"]

                def f_zadd(e, tco=tco, g=g, gld_o=gld_o):
                    em.wait(e, "vector", "gld", gld_o)
                    em.inc(e.tensor_add(z_g(g, tco), z_g(g, tco), st2b[:, 0:TC]),
                           "ve", 1)
                on("vector", f_zadd)
                zup_ve = em.counts["ve"]
        zup_done = em.counts["ve"]

        RMS2C = bv(O_RMS2C, 4096)
        HIDv = bv(O_HID, 16384)
        FROW = fv(O_FROW, 2048)
        out_st = []
        fin_st = []
        up_gld = {}
        ffn_prev = {}
        for tco in range(NCHO):
            def f_xafl(e, tco=tco):
                em.wait(e, "sync", "st", st_pageout)
                em.wait(e, "sync", "ve", zup_done)
                if tco >= 1:
                    em.wait(e, "sync", "ve", ffn_prev.get("final_ve", 0))
                for g in range(G8):
                    em.inc(e.dma_start(out=fv(O_XAFC + g * TC, TC),
                                       in_=xaf_d.ap()[g * 128:(g + 1) * 128, tco * TC:(tco + 1) * TC]),
                           "ld", 16)
            on("sync", f_xafl)
            ld_xafc = em.counts["ld"]

            sq3 = []
            for g in range(G8):
                sslot = st4b[:, (g % 2) * TC:(g % 2) * TC + TC]

                def f_sq8(e, tco=tco, g=g, sslot=sslot):
                    em.wait(e, "scalar", "ve", zup_done)
                    if g >= 2:
                        em.wait(e, "scalar", "mm", sq3[g - 2])
                    elif tco >= 1:
                        em.wait(e, "scalar", "ve", ffn_prev["dn_ve_last"])
                        em.wait(e, "scalar", "ve", ffn_prev["final_ve"])
                    em.inc(e.activation(out=sslot, in_=z_g(g, tco), func=AF.Square),
                           "sc", 1)
                on("scalar", f_sq8)
                sc_s8 = em.counts["sc"]

                def f_mm8(e, g=g, sslot=sslot, sc_s8=sc_s8, tco=tco):
                    em.wait(e, "tensor", "sc", sc_s8)
                    if tco >= 1 and g == 0:
                        em.wait(e, "tensor", "sc", ffn_prev["dnev_sc"])
                    em.inc(e.matmul(out=pb[4][0:1, :], lhsT=s_onesc[:, :], rhs=sslot,
                                    start=(g == 0), stop=(g == G8 - 1)), "mm", 1)
                on("tensor", f_mm8)
                sq3.append(em.counts["mm"])
            mm_s8 = em.counts["mm"]

            def f_rs2(e, mm_s8=mm_s8):
                em.wait(e, "scalar", "mm", mm_s8)
                em.inc(e.activation(out=rows_t, in_=pb[4][0:1, :], func=AF.Sqrt,
                                    scale=1.0 / D, bias=np.float32(1e-6)), "sc", 1)
            on("scalar", f_rs2)
            sc_rs2 = em.counts["sc"]

            def f_rs2v(e, sc_rs2=sc_rs2):
                em.wait(e, "vector", "sc", sc_rs2)
                em.inc(e.reciprocal(out=rows_r, in_=rows_t), "ve", 1)
            on("vector", f_rs2v)
            ve_rs2 = em.counts["ve"]

            def f_repr2(e, ve_rs2=ve_rs2):
                em.wait(e, "tensor", "ve", ve_rs2)
                em.inc(e.matmul(out=pb[6][:, :], lhsT=s_ones1[:, :], rhs=rows_r,
                                start=True, stop=True), "mm", 1)
            on("tensor", f_repr2)
            mm_repr2 = em.counts["mm"]

            def f_r2n(e, tco=tco, mm_repr2=mm_repr2):
                em.wait(e, "vector", "mm", mm_repr2)
                last = None
                for g in range(G8):
                    last = e.scalar_tensor_tensor(
                        out=RMS2C[:, g * TC:(g + 1) * TC], in0=z_g(g, tco),
                        scalar=s_n2wg[:, g:g + 1], in1=pb[6][:, :],
                        op0=OP.mult, op1=OP.mult)
                em.inc(last, "ve", 1)
            on("vector", f_r2n)
            ve_r2n = em.counts["ve"]

            gl_sc = {}
            L = lambda lane, a, b: s_g8[:, lane * 32 + a:lane * 32 + b]
            for ts in range(4):
                def f_gl(e, tco=tco, ts=ts, ve_r2n=ve_r2n):
                    em.wait(e, "tensor", "ve", ve_r2n)
                    if ts >= 1:
                        em.wait(e, "tensor", "sc", gl_sc[ts - 1])
                    last = None
                    for kg in range(G8):
                        last = e.matmul(
                            out=pb[5][0:128, 0:8],
                            lhsT=RMS2C[:, kg * TC + ts * 128: kg * TC + ts * 128 + 128],
                            rhs=bv(O_GW, 128)[:, kg * 8:(kg + 1) * 8],
                            start=(kg == 0), stop=(kg == G8 - 1))
                    em.inc(last, "mm", 1)
                on("tensor", f_gl)
                mm_gl = em.counts["mm"]

                def f_gle(e, ts=ts, mm_gl=mm_gl, tco=tco):
                    em.wait(e, "scalar", "mm", mm_gl)
                    if ts == 0 and tco >= 1:
                        em.wait(e, "scalar", "ve", ffn_prev["gtop_ve"])
                    em.inc(e.activation(out=L(ts, 0, 8), in_=pb[5][0:128, 0:8],
                                        func=AF.Copy), "sc", 1)
                on("scalar", f_gle)
                gl_sc[ts] = em.counts["sc"]

            def f_gmax(e, tco=tco):
                em.wait(e, "vector", "sc", gl_sc[3])
                for lane in range(4):
                    e.tensor_reduce(out=L(lane, 16, 17), in_=L(lane, 0, 8),
                                    axis=AX.X, op=OP.max)
                last = None
                for lane in range(4):
                    last = e.tensor_sub(L(lane, 17, 18), s_gc[:, 0:1], L(lane, 16, 17))
                em.inc(last, "ve", 1)
            on("vector", f_gmax)
            ve_gm = em.counts["ve"]

            def f_gexp(e, tco=tco, ve_gm=ve_gm):
                em.wait(e, "scalar", "ve", ve_gm)
                last = None
                for lane in range(4):
                    last = e.activation(out=L(lane, 8, 16), in_=L(lane, 0, 8),
                                        func=AF.Exp, bias=L(lane, 17, 18),
                                        accum_out=L(lane, 18, 19))
                em.inc(last, "sc", 1)
            on("scalar", f_gexp)
            sc_ge = em.counts["sc"]

            def f_gtop(e, tco=tco, sc_ge=sc_ge):
                em.wait(e, "vector", "sc", sc_ge)
                for lane in range(4):
                    e.tensor_tensor(out=L(lane, 20, 28), in0=L(lane, 8, 16),
                                    in1=s_gc[:, 2:3].to_broadcast([128, 8]),
                                    op=OP.is_ge)
                for lane in range(4):
                    e.tensor_add(L(lane, 20, 28), L(lane, 20, 28), L(lane, 20, 28))
                for lane in range(4):
                    e.tensor_sub(L(lane, 20, 28), L(lane, 8, 16), L(lane, 20, 28))
                for lane in range(4):
                    e.tensor_reduce(out=L(lane, 28, 29), in_=L(lane, 20, 28),
                                    axis=AX.X, op=OP.max)
                for lane in range(4):
                    e.reciprocal(out=L(lane, 19, 20), in_=L(lane, 18, 19))
                for lane in range(4):
                    e.tensor_add(L(lane, 28, 29), s_gc[:, 2:3], L(lane, 28, 29))
                for lane in range(4):
                    e.tensor_mul(L(lane, 29, 30), L(lane, 28, 29), L(lane, 19, 20))
                for lane in range(4):
                    e.tensor_add(L(lane, 30, 31), L(lane, 29, 30), s_gc[:, 1:2])
                for lane in range(4):
                    e.reciprocal(out=L(lane, 31, 32), in_=L(lane, 30, 31))
                last = None
                for lane in range(4):
                    last = e.tensor_mul(s_fcol[:, lane:lane + 1],
                                        L(lane, 29, 30), L(lane, 31, 32))
                em.inc(last, "ve", 1)
            on("vector", f_gtop)
            ffn_prev["gtop_ve"] = em.counts["ve"]
            ve_f = em.counts["ve"]

            def f_fd(e, tco=tco, ve_f=ve_f):
                em.wait(e, "gpsimd", "ve", ve_f)
                for ts in range(4):
                    em.inc(e.dma_start(
                        out=FROW[0:1, tco * TC + ts * 128: tco * TC + (ts + 1) * 128],
                        in_=s_fcol[:, ts:ts + 1],
                        allow_slow_non_contiguous=True), "tr", 16)
            on("gpsimd", f_fd)
            gld_f = em.counts["tr"]

            hev_sc = {}
            for hs in range(2):
                def f_upl(e, tco=tco, hs=hs):
                    if tco == 0 and hs == 0:
                        em.wait(e, "gpsimd", "rst", precast_done)
                    if tco >= 1 or hs >= 1:
                        em.wait(e, "gpsimd", "mm", up_gld.get("mm_prev", 0))
                    for kg in range(G8):
                        em.inc(e.dma_start(
                            out=bv(O_UP0 + kg * 1024, 2048),
                            in_=up16.ap()[kg * 128:(kg + 1) * 128, hs * 2048:(hs + 1) * 2048]),
                            "wld", 16)
                on("gpsimd", f_upl)
                gld_up = em.counts["wld"]

                for hb in range(2):
                    def f_hmm(e, tco=tco, hs=hs, hb=hb, gld_up=gld_up, ve_r2n=ve_r2n):
                        em.wait(e, "tensor", "wld", gld_up)
                        em.wait(e, "tensor", "ve", ve_r2n)
                        if (hs, hb) != (0, 0):
                            em.wait(e, "tensor", "sc", hev_sc[prev_hb])
                        else:
                            em.wait(e, "tensor", "sc", gl_sc[3])
                        for hgi in range(8):
                            last = None
                            for kg in range(G8):
                                last = e.matmul(
                                    out=pb[hgi][:, :],
                                    lhsT=bv(O_UP0 + kg * 1024, 2048)[:, hb * 1024 + hgi * 128: hb * 1024 + hgi * 128 + 128],
                                    rhs=RMS2C[:, kg * TC:(kg + 1) * TC],
                                    start=(kg == 0), stop=(kg == G8 - 1))
                            em.inc(last, "mm", 1)
                    on("tensor", f_hmm)
                    mm_h = em.counts["mm"]

                    def f_hev(e, tco=tco, hs=hs, hb=hb, mm_h=mm_h):
                        em.wait(e, "scalar", "mm", mm_h)
                        last = None
                        for hgi in range(8):
                            hg = hs * 16 + hb * 8 + hgi
                            last = e.activation(out=HIDv[:, hg * TC:(hg + 1) * TC],
                                                in_=pb[hgi][:, :], func=AF.Silu)
                        em.inc(last, "sc", 1)
                    on("scalar", f_hev)
                    hev_sc[(hs, hb)] = em.counts["sc"]
                    prev_hb = (hs, hb)
                up_gld["mm_prev"] = em.counts["mm"]
            hid_sc = em.counts["sc"]

            def f_freprep(e, tco=tco, gld_f=gld_f):
                em.wait(e, "tensor", "tr", gld_f)
                em.wait(e, "tensor", "sc", hid_sc)
                em.inc(e.matmul(out=pb[7][:, :], lhsT=s_ones1[:, :],
                                rhs=FROW[0:1, tco * TC:(tco + 1) * TC],
                                start=True, stop=True), "mm", 1)
            on("tensor", f_freprep)
            mm_fr = em.counts["mm"]

            def f_frev(e, mm_fr=mm_fr):
                em.wait(e, "scalar", "mm", mm_fr)
                em.inc(e.activation(out=s_stage3[:, :], in_=pb[7][:, :], func=AF.Copy),
                       "sc", 1)
            on("scalar", f_frev)
            sc_fr = em.counts["sc"]

            dn_mm = {}
            for kh in range(32):
                pqd = "pqA" if kh % 2 == 0 else "pqB"

                def f_dnl(e, tco=tco, kh=kh, pqd=pqd):
                    if tco == 0 and kh < 2:
                        em.wait(e, "gpsimd", "rst", precast_done)
                    if kh >= 2 or tco >= 1:
                        em.wait(e, "gpsimd", "mm", dn_mm.get(kh - 2, ffn_prev.get("dn_mm_last", 0)))
                    em.inc(e.dma_start(
                        out=bv(O_DN0 if kh % 2 == 0 else O_DN1, 1024),
                        in_=dn16.ap()[kh * 128:(kh + 1) * 128, :]), pqd, 16)
                on("gpsimd", f_dnl)
                gld_dn = em.counts[pqd]

                def f_dmm(e, tco=tco, kh=kh, gld_dn=gld_dn, hid_sc=hid_sc, pqd=pqd):
                    em.wait(e, "tensor", pqd, gld_dn)
                    em.wait(e, "tensor", "sc", hid_sc)
                    if kh == 0:
                        em.wait(e, "tensor", "sc", sc_fr)
                    last = None
                    for og in range(G8):
                        last = e.matmul(
                            out=pb[og][:, :],
                            lhsT=bv(O_DN0 if kh % 2 == 0 else O_DN1, 1024)[:, og * 128:(og + 1) * 128],
                            rhs=HIDv[:, kh * TC:(kh + 1) * TC],
                            start=(kh == 0), stop=(kh == 31))
                    em.inc(last, "mm", 1)
                on("tensor", f_dmm)
                dn_mm[kh] = em.counts["mm"]
            ffn_prev["dn_mm_last"] = dn_mm[31]

            dn_sc_last = 0
            for og in range(G8):
                def f_dev(e, tco=tco, og=og):
                    em.wait(e, "scalar", "mm", dn_mm[31])
                    em.wait(e, "scalar", "ve", dn_ve.get(og - 2, 0))
                    if og < 2:
                        em.wait(e, "scalar", "st", ffn_prev.get("fin_st", 0))
                    em.inc(e.activation(out=st2b[:, (og % 2) * TC:(og % 2) * TC + TC],
                                        in_=pb[og][:, :], func=AF.Copy), "sc", 1)
                dn_ve = {} if og == 0 else dn_ve
                on("scalar", f_dev)
                sc_d = em.counts["sc"]

                def f_dz(e, tco=tco, og=og, sc_d=sc_d):
                    em.wait(e, "vector", "sc", sc_d)
                    e.tensor_tensor(out=s_stage4[:, :],
                                    in0=st2b[:, (og % 2) * TC:(og % 2) * TC + TC],
                                    in1=s_stage3[:, :], op=OP.mult)
                    em.inc(e.tensor_add(z_g(og, tco), z_g(og, tco), s_stage4[:, :]),
                           "ve", 1)
                on("vector", f_dz)
                dn_ve[og] = em.counts["ve"]
            ffn_prev["dnev_sc"] = em.counts["sc"]
            ffn_prev["dn_ve_last"] = dn_ve[G8 - 1]

            for g in range(G8):
                def f_fin(e, tco=tco, g=g, ld_xafc=ld_xafc):
                    em.wait(e, "vector", "ld", ld_xafc)
                    em.wait(e, "vector", "ve", dn_ve[G8 - 1])
                    if len(fin_st) >= 2:
                        em.wait(e, "vector", "st", fin_st[-2])
                    em.inc(e.tensor_add((s_stage if g % 2 == 0 else s_stage2)[:, :],
                                        fv(O_XAFC + g * TC, TC), z_g(g, tco)), "ve", 1)
                on("vector", f_fin)
                ve_fin = em.counts["ve"]

                def f_find(e, tco=tco, g=g, ve_fin=ve_fin):
                    em.wait(e, "sync", "ve", ve_fin)
                    em.inc(e.dma_start(
                        out=out_fm[g * 128:(g + 1) * 128, tco * TC:(tco + 1) * TC],
                        in_=(s_stage if g % 2 == 0 else s_stage2)[:, :]), "st", 16)
                on("sync", f_find)
                fin_st.append(em.counts["st"])
                out_st.append(em.counts["st"])
            ffn_prev["final_ve"] = em.counts["ve"]
            ffn_prev["fin_st"] = fin_st[-1]

        def f_tail(e):
            em.wait(e, "sync", "st", max(out_st))
        on("sync", f_tail)

    return nc


_NC_CACHE = {}


def _get_nc():
    if "nc" not in _NC_CACHE:
        _NC_CACHE["nc"] = build_program()
    return _NC_CACHE["nc"]


def kernel(L0_attn_x, L0_attn_W_in, L0_attn_b_in, L0_attn_conv_w, L0_attn_conv_b,
           L0_attn_W_x, L0_attn_b_x, L0_attn_A_log, L0_attn_D_param,
           L0_attn_W_out, L0_attn_b_out, L0_ffn_n1_w, L0_ffn_qkv_w, L0_ffn_o_w,
           L0_ffn_n2_w, L0_ffn_gate_w, L0_ffn_up_w, L0_ffn_down_w):
    f32 = np.float32
    x = np.asarray(L0_attn_x, f32)
    W_in = np.asarray(L0_attn_W_in, f32)
    b_in = np.asarray(L0_attn_b_in, f32)
    conv_w = np.asarray(L0_attn_conv_w, f32)
    conv_b = np.asarray(L0_attn_conv_b, f32)
    W_x = np.asarray(L0_attn_W_x, f32)
    b_x = np.asarray(L0_attn_b_x, f32)
    A_log = np.asarray(L0_attn_A_log, f32)
    D_param = np.asarray(L0_attn_D_param, f32)
    W_out = np.asarray(L0_attn_W_out, f32)
    b_out = np.asarray(L0_attn_b_out, f32)
    n1_w = np.asarray(L0_ffn_n1_w, f32)
    qkv_w = np.asarray(L0_ffn_qkv_w, f32)
    o_w = np.asarray(L0_ffn_o_w, f32)
    n2_w = np.asarray(L0_ffn_n2_w, f32)
    gate_w = np.asarray(L0_ffn_gate_w, f32)
    up_w = np.asarray(L0_ffn_up_w, f32)
    down_w = np.asarray(L0_ffn_down_w, f32)

    def grp(v, ng):
        return np.ascontiguousarray(v.reshape(ng, 128).T)

    qkvT_full = np.ascontiguousarray(qkv_w.T)
    owT_full = np.ascontiguousarray(o_w.T)
    upT_full = np.ascontiguousarray(up_w.T)
    dnT_full = np.ascontiguousarray(down_w.T)
    gwT_full = np.ascontiguousarray(gate_w.T)
    triu_full = np.ascontiguousarray(np.triu(np.ones((128, 128), f32), 1))
    negeye_full = np.ascontiguousarray(-1e9 * np.eye(128, dtype=f32))

    in_maps = []
    for c in range(8):
        b = c // 2
        h = c % 2
        sl = slice(512 * h, 512 * h + 512)
        xTb = np.ascontiguousarray(x[b].T)
        x_ownT = np.ascontiguousarray(x[b, 2048 * h:2048 * h + 2048, :].T)
        winT_own = np.ascontiguousarray(
            np.concatenate([W_in[:, sl], W_in[:, 1024 + 512 * h:1024 + 512 * h + 512]], axis=1))
        binr_own = np.ascontiguousarray(
            np.concatenate([b_in[sl], b_in[1024 + 512 * h:1024 + 512 * h + 512]])[None, :])
        convw_own = np.ascontiguousarray(
            conv_w[sl].reshape(4, 128, 4).transpose(1, 0, 2).reshape(128, 16))
        convb_own = np.ascontiguousarray(conv_b[sl].reshape(4, 128).T)
        alog_own = np.ascontiguousarray(
            A_log[sl].reshape(4, 128, 16).transpose(1, 0, 2).reshape(128, 64))
        dpar_own = np.ascontiguousarray(D_param[sl].reshape(4, 128).T)
        wx_own = np.ascontiguousarray(W_x[sl, :])
        wout_own = np.ascontiguousarray(W_out[sl, :])
        qkvT_own = np.ascontiguousarray(np.concatenate(
            [qkvT_full[:, sl], qkvT_full[:, 1024 + 512 * h:1024 + 512 * h + 512],
             qkvT_full[:, 2048 + 512 * h:2048 + 512 * h + 512]], axis=1))
        owT_own = np.ascontiguousarray(owT_full[sl, :])
        in_maps.append({
            "xT": xTb,
            "x_ownT": x_ownT,
            "winT": winT_own,
            "binr": binr_own,
            "convw": convw_own,
            "convb": convb_own,
            "alog": alog_own,
            "dparg": dpar_own,
            "wxT": wx_own,
            "bxc": np.ascontiguousarray(b_x[:, None]),
            "woutT": wout_own,
            "boutg": grp(b_out, 8),
            "n1wg": grp(n1_w, 8),
            "n2wg": grp(n2_w, 8),
            "qkvT": qkvT_own,
            "owT": owT_own,
            "gwT": gwT_full,
            "upT": upT_full,
            "dnT": dnT_full,
            "triu1": triu_full,
            "negeye": negeye_full,
        })

    nc = _get_nc()
    res = run_bass_kernel_spmd(nc, in_maps, core_ids=list(range(8)))

    out = np.empty((4, T, D), f32)
    for c in range(8):
        b = c // 2
        h = c % 2
        shard = np.asarray(res.results[c]["out_fm"])
        out[b, 2048 * h:2048 * h + 2048, :] = shard.T
    return out


# version 17 (final)
# speedup vs baseline: 1.0315x; 1.0315x over previous
import contextlib
import numpy as np
import concourse.bass as bass
import concourse.mybir as mybir
from concourse.bass_utils import run_bass_kernel_spmd

F32 = mybir.dt.float32
BF16 = mybir.dt.bfloat16
OP = mybir.AluOpType
AF = mybir.ActivationFunctionType
AX = mybir.AxisListType

T = 4096
D = 1024
DH = 512
N = 16
HD = 64
ED = 4096
TH = 2048
TC = 512
NCH = 8
NCHO = 4
G8 = 8
G4 = 4
RINGM = 3
NKT = 32
GROUPS = [[0, 1], [2, 3], [4, 5], [6, 7]]
ARN = 42560

O_XS = 0
O_XNA = 2060
O_XNB = 2572
O_XBF = 3084
O_LN1 = 5132
O_H = 7180
O_U = 15372
O_REP = 19468
O_XCRR = 27660
O_GTRR = 30732
WREG = 33804
O_WIN = WREG
O_XAF = 0
O_Z = 16384
O_QKV = 32768
O_QKVV = 0
O_OW = 32768
O_UP0 = 32768
O_DN0 = 40960
O_DN1 = 41984
O_GW = 42496
O_RATT = 0
O_RMS1C = 8192
O_Q2 = 10240
O_C1M = 14336
O_KP = 8192
O_VT0 = 10240
O_VT1 = 11280
O_QT = 12320
O_P0 = 12576
O_P1 = 13088
O_AV = 13600
O_AVN = 14112
O_XAFC = 0
O_HID = 4096
O_RMS2C = 12288
O_FROW = 14336


class Emitter:
    def __init__(self, nc, stack):
        self.nc = nc
        self.sems = {}
        self.counts = {}
        self.seen = {}
        self.stack = stack

    def sem(self, name):
        self.sems[name] = self.stack.enter_context(self.nc.semaphore(name))
        self.counts[name] = 0

    def inc(self, instr, name, k=1):
        instr.then_inc(self.sems[name], k)
        self.counts[name] += k
        return self.counts[name]

    def wait(self, eng, ename, name, val=None):
        v = self.counts[name] if val is None else val
        if v <= 0:
            return
        key = (ename, name)
        if self.seen.get(key, -1) >= v:
            return
        eng.wait_ge(self.sems[name], v)
        self.seen[key] = v


def build_program():
    nc = bass.Bass("TRN2", target_bir_lowering=False)

    def din(name, shape, dt=F32):
        return nc.dram_tensor(name, shape, dt, kind="ExternalInput").ap()

    xT = din("xT", [D, T])
    x_ownT = din("x_ownT", [D, TH])
    winT = din("winT", [D, 2 * DH])
    binr = din("binr", [1, 2 * DH])
    convw = din("convw", [128, G4 * 4])
    convb = din("convb", [128, G4])
    alog = din("alog", [128, G4 * N])
    dparg = din("dparg", [128, G4])
    wxT = din("wxT", [DH, 32])
    bxc = din("bxc", [32, 1])
    woutT = din("woutT", [DH, D])
    boutg = din("boutg", [128, G8])
    n1wg = din("n1wg", [128, G8])
    n2wg = din("n2wg", [128, G8])
    qkvT = din("qkvT", [D, 3 * DH])
    owT = din("owT", [DH, D])
    gwT = din("gwT", [D, 8])
    upT = din("upT", [D, ED])
    dnT = din("dnT", [ED, D])
    triu1 = din("triu1", [128, 128])
    negeye = din("negeye", [128, 128])

    out_fm = nc.dram_tensor("out_fm", [D, TH], F32, kind="ExternalOutput").ap()

    def dint(name, shape, dt):
        return nc.dram_tensor(name, shape, dt)

    bc_in = [dint(f"bc_in{c}", [32, TC], F32) for c in range(NCH)]
    bc_out = [dint(f"bc_out{c}", [32, TC], F32) for c in range(NCH)]
    wo_in = dint("wo_in", [2, D, TH], BF16)
    wo_own = dint("wo_own", [1, D, TH], BF16)
    rms1_in = dint("rms1_in", [1, D, TH], BF16)
    rms1_out = dint("rms1_out", [2, D, TH], BF16)
    op_in = dint("op_in", [2, D, TH], BF16)
    op_own = dint("op_own", [1, D, TH], BF16)
    qQ = dint("qQ", [DH, T], BF16)
    qK = dint("qK", [DH, T], BF16)
    qV = dint("qV", [T, DH], BF16)
    xaf_d = dint("xaf_d", [D, TH], F32)

    with contextlib.ExitStack() as stack:
        E = stack.enter_context
        em = Emitter(nc, stack)
        for s in ["ld", "gld", "mm", "sc", "ve", "gp", "cc", "st",
                  "tr", "wld", "kvl", "bst", "wst", "rst", "xbl", "pqA", "pqB",
                  "mcx"]:
            em.sem(s)

        AR = E(nc.sbuf_tensor("AR", [128, ARN], F32))
        ARF = AR[:, :]
        ARB = ARF.bitcast(BF16)

        def fv(off, n):
            return ARF[:, off:off + n]

        def bv(off, n):
            return ARB[:, 2 * off:2 * off + n]

        s_convw = E(nc.sbuf_tensor("s_convw", [128, G4 * 4], F32))
        s_convb = E(nc.sbuf_tensor("s_convb", [128, G4], F32))
        s_A = E(nc.sbuf_tensor("s_A", [128, G4 * N], F32))
        s_dpar = E(nc.sbuf_tensor("s_dpar", [128, G4], F32))
        s_carry = E(nc.sbuf_tensor("s_carry", [128, G4 * N], F32))
        s_bxc = E(nc.sbuf_tensor("s_bxc", [32, 1], F32))
        s_boutg = E(nc.sbuf_tensor("s_boutg", [128, G8], F32))
        s_n1wg = E(nc.sbuf_tensor("s_n1wg", [128, G8], F32))
        s_n2wg = E(nc.sbuf_tensor("s_n2wg", [128, G8], F32))
        s_ones1 = E(nc.sbuf_tensor("s_ones1", [1, 128], F32))
        s_onesc = E(nc.sbuf_tensor("s_onesc", [128, 1], BF16))
        s_onesr = E(nc.sbuf_tensor("s_onesr", [1, TC], BF16))
        s_triu = E(nc.sbuf_tensor("s_triu", [128, 128], BF16))
        s_negi = E(nc.sbuf_tensor("s_negi", [128, 128], BF16))
        s_hsel = E(nc.sbuf_tensor("s_hsel", [128, 2], BF16))
        s_kmx = E(nc.sbuf_tensor("s_kmx", [128, G4], F32))
        s_fcol = E(nc.sbuf_tensor("s_fcol", [128, G4], F32))
        s_c1r = E(nc.sbuf_tensor("s_c1r", [128, 64], F32))
        s_g8 = E(nc.sbuf_tensor("s_g8", [128, 128], F32))
        s_gc = E(nc.sbuf_tensor("s_gc", [128, 4], F32))
        s_binr = E(nc.sbuf_tensor("s_binr", [1, 2 * DH], BF16))
        s_wx = E(nc.sbuf_tensor("s_wx", [128, G4 * 32], BF16))
        s_rows2 = E(nc.sbuf_tensor("s_rows2", [32, TC], F32))
        s_wout = E(nc.sbuf_tensor("s_wout", [128, G4 * 1024], BF16))
        s_stage = E(nc.sbuf_tensor("s_stage", [128, TC], F32))
        s_stage2 = E(nc.sbuf_tensor("s_stage2", [128, TC], F32))
        s_stage3 = E(nc.sbuf_tensor("s_stage3", [128, TC], F32))
        s_stage4 = E(nc.sbuf_tensor("s_stage4", [128, TC], F32))
        ROWS = E(nc.sbuf_tensor("rows", [1, 2048], F32))
        YB = E(nc.sbuf_tensor("yb", [128, 2 * G4 * TC], BF16))

        st2b = s_stage2[:, :].bitcast(BF16)
        st3b = s_stage3[:, :].bitcast(BF16)
        st4b = s_stage4[:, :].bitcast(BF16)

        pb = [E(nc.psum_tensor(f"pb{i}", [128, TC], F32)) for i in range(8)]

        blk = E(nc.Block())

        def on(engine_name, fn):
            getattr(blk, engine_name)(fn)

        rows_m = ROWS[0:1, 0:TC]
        rows_r = ROWS[0:1, TC:2 * TC]
        rows_mr = ROWS[0:1, 2 * TC:3 * TC]
        rows_t = ROWS[0:1, 3 * TC:4 * TC]
        rows_t2 = rows_mr

        def ph0_sync(e):
            for dst, src in [
                (s_convw[:, :], convw), (s_convb[:, :], convb),
                (s_A[:, :], alog), (s_dpar[:, :], dparg),
                (s_bxc[:, :], bxc),
                (s_boutg[:, :], boutg), (s_n1wg[:, :], n1wg),
                (s_n2wg[:, :], n2wg),
            ]:
                em.inc(e.dma_start(out=dst, in_=src), "ld", 16)
        on("sync", ph0_sync)
        ld0 = em.counts["ld"]

        def ph0_gp(e):
            for g in range(G8):
                em.inc(e.dma_start(out=bv(O_WIN, 16384)[:, g * 1024:(g + 1) * 1024],
                                   in_=winT[g * 128:(g + 1) * 128, :]), "gld", 16)
            for g in range(G4):
                em.inc(e.dma_start(out=s_wx[:, g * 32:(g + 1) * 32],
                                   in_=wxT[g * 128:(g + 1) * 128, :]), "gld", 16)
            for g in range(G4):
                em.inc(e.dma_start(out=s_wout[:, g * 1024:(g + 1) * 1024],
                                   in_=woutT[g * 128:(g + 1) * 128, :]), "gld", 16)
            em.inc(e.dma_start(out=s_binr[:, :], in_=binr), "gld", 16)
            em.inc(e.dma_start(out=s_triu[:, :], in_=triu1), "gld", 16)
            em.inc(e.dma_start(out=s_negi[:, :], in_=negeye), "gld", 16)
            e.memset(s_ones1[:, :], 1.0)
            for g in range(G4):
                e.memset(ARF[:, O_XS + g * 515:O_XS + g * 515 + 3], 0.0)
            e.memset(s_gc[:, 0:1], 0.0)
            e.memset(s_gc[:, 1:2], 1e-9)
            e.memset(s_gc[:, 2:3], 1.0)
            e.memset(s_onesc[:, :], 1.0)
            e.memset(s_onesr[:, :], 1.0)
            e.memset(s_carry[:, :], 0.0)
            e.memset(s_kmx[:, :], 0.0)
            e.memset(s_hsel[:, :], 0.0)
            e.memset(s_hsel[0:64, 0:1], 1.0)
            em.inc(e.memset(s_hsel[64:128, 1:2], 1.0), "gp", 1)
        on("gpsimd", ph0_gp)
        gld0 = em.counts["gld"]

        def ph0_scalar(e):
            em.wait(e, "scalar", "ld", ld0)
            e.activation(out=s_A[:, :], in_=s_A[:, :], func=AF.Exp)
            e.activation(out=s_A[:, :], in_=s_A[:, :], func=AF.Exp, scale=-1.0)
            e.mul(out=s_bxc[:, :], in_=s_bxc[:, :], mul=0.5)
            em.inc(e.mul(out=s_boutg[:, :], in_=s_boutg[:, :], mul=0.5), "sc", 1)
        on("scalar", ph0_scalar)
        sc0 = em.counts["sc"]

        xs_g = lambda g: fv(O_XS + g * 515, 515)
        xbf_g = lambda g: bv(O_XBF, 4096)[:, g * TC:(g + 1) * TC]
        ln1_g = lambda g: bv(O_LN1, 4096)[:, g * TC:(g + 1) * TC]
        H_all = fv(O_H, 8192)
        H_n = lambda n: fv(O_H + n * TC, TC)
        U_all = bv(O_U, 8192)
        U_n = lambda n: bv(O_U, 8192)[:, n * TC:(n + 1) * TC]
        REP = bv(O_REP, 16384)
        REPB = bv(O_REP, 8192)
        REPC = bv(O_REP + 4096, 8192)
        wvb = bv(O_XNA, 2048)
        wv_slot = lambda i: wvb[:, i * TC:(i + 1) * TC]

        def xcr(tc, g):
            r = tc % RINGM
            return bv(O_XCRR, RINGM * G4 * TC)[:, (r * G4 + g) * TC:(r * G4 + g + 1) * TC]

        def gtr(tc, g):
            r = tc % RINGM
            return bv(O_GTRR, RINGM * G4 * TC)[:, (r * G4 + g) * TC:(r * G4 + g + 1) * TC]

        def ybr(tc, g):
            r = tc % 2
            return YB[:, (r * G4 + g) * TC:(r * G4 + g + 1) * TC]

        statmm = {}
        scan_done = {}
        ygate_done = {}
        wout_mm = {}
        bcevac_sc = {}
        gate_evac_sc = {}
        wout_evac_sc = {}
        conv_done = {}
        bc_st = {}
        wo_ev_st = {}
        ar_cc = {}
        wo_st = []
        silu_prev = 0
        norm_prev_last = 0
        prev_mm_rep = 0
        prev_ve_r4a = 0
        back_ctx = {}

        def emit_scan(tc):
            tr_tree, sc_xc = back_ctx[tc]
            assert scan_done_static[tc] == em.counts["ve"] + 4, \
                (tc, scan_done_static[tc], em.counts["ve"])

            def f_scan(e, tc=tc, tr_tree=tr_tree, sc_xc=sc_xc):
                em.wait(e, "vector", "tr", tr_tree)
                em.wait(e, "vector", "sc", sc_xc)
                em.wait(e, "vector", "sc", gate_evac_sc[tc])
                if tc >= 2:
                    em.wait(e, "vector", "mm", wout_mm[tc - 2])
                for g in range(G4):
                    e.tensor_tensor(
                        out=U_all.rearrange("p (n t) -> p n t", n=N),
                        in0=xcr(tc, g).unsqueeze(1).to_broadcast([128, N, TC]),
                        in1=REPB.rearrange("p (n t) -> p n t", n=N),
                        op=OP.mult)
                    for n in range(N):
                        e.tensor_tensor_scan(
                            out=H_n(n),
                            data0=s_A[:, g * N + n:g * N + n + 1].to_broadcast([128, TC]),
                            data1=U_n(n),
                            initial=s_carry[:, g * N + n:g * N + n + 1],
                            op0=OP.mult, op1=OP.add)
                    e.tensor_copy(
                        s_carry[:, g * N:g * N + N - 1],
                        H_all.rearrange("p (n t) -> p n t", n=N)[:, 0:N - 1, TC - 1:TC])
                    e.tensor_copy(
                        s_carry[:, g * N + N - 1:g * N + N],
                        H_all[:, (N - 1) * TC + TC - 1:(N - 1) * TC + TC])
                    mc = e.tensor_tensor(out=H_all, in0=H_all, in1=REPC, op=OP.mult)
                    if g == G4 - 1:
                        em.inc(mc, "mcx", 1)
                    for half in (8, 4, 2):
                        hv = fv(O_H, 2 * half * TC).rearrange("p (a b t) -> p a b t",
                                                              a=half, b=2)
                        e.tensor_tensor(
                            out=fv(O_H, half * TC).rearrange("p (a t) -> p a t", a=half),
                            in0=hv[:, :, 0, :], in1=hv[:, :, 1, :], op=OP.add)
                    e.tensor_tensor(out=s_stage[:, :], in0=H_n(0), in1=H_n(1),
                                    op=OP.add)
                    e.scalar_tensor_tensor(
                        out=s_stage[:, :], in0=xcr(tc, g),
                        scalar=s_dpar[:, g:g + 1], in1=s_stage[:, :],
                        op0=OP.mult, op1=OP.add)
                    em.inc(e.tensor_tensor(out=ybr(tc, g), in0=s_stage[:, :],
                                           in1=gtr(tc, g), op=OP.mult), "ve", 1)
            on("vector", f_scan)
            scan_done[tc] = em.counts["ve"]
            ygate_done[tc] = em.counts["ve"]
            assert scan_done_static[tc] == em.counts["ve"]

        def emit_wout(tc):
            def f_wout_a(e, tc=tc):
                em.wait(e, "tensor", "ve", scan_done[tc])
                em.wait(e, "tensor", "sc", bcevac_sc[min(tc + 3, NCH - 1)])
                if tc >= 1:
                    em.wait(e, "tensor", "sc", wout_evac_sc[tc - 1])
                last = None
                for og in range(G4):
                    for kg in range(G4):
                        last = e.matmul(
                            out=pb[og][:, :],
                            lhsT=s_wout[:, kg * 1024 + og * 128: kg * 1024 + og * 128 + 128],
                            rhs=ybr(tc, kg), start=(kg == 0), stop=(kg == G4 - 1))
                em.inc(last, "mm", 1)
            on("tensor", f_wout_a)
            mm_wa = em.counts["mm"]

            def f_wout_ea(e, tc=tc, mm_wa=mm_wa):
                em.wait(e, "scalar", "mm", mm_wa)
                em.wait(e, "scalar", "wst", wo_ev_st.get((tc - 1, 1), 0))
                last = None
                for og in range(G4):
                    last = e.activation(out=wv_slot(og), in_=pb[og][:, :],
                                        func=AF.Identity, bias=s_boutg[:, og:og + 1])
                em.inc(last, "sc", 1)
            on("scalar", f_wout_ea)
            sc_wea = em.counts["sc"]

            def f_wout_da(e, tc=tc, sc_wea=sc_wea):
                em.wait(e, "sync", "sc", sc_wea)
                half = tc // 4
                col = (tc % 4) * TC
                for og in range(G4):
                    em.inc(e.dma_start(
                        out=wo_in.ap()[half:half + 1, og * 128:(og + 1) * 128, col:col + TC],
                        in_=wv_slot(og)), "wst", 16)
            on("sync", f_wout_da)
            wo_ev_st[(tc, 0)] = em.counts["wst"]

            def f_wout_b(e, tc=tc, sc_wea=sc_wea):
                em.wait(e, "tensor", "sc", sc_wea)
                last = None
                for og in range(4, G8):
                    for kg in range(G4):
                        last = e.matmul(
                            out=pb[og - 4][:, :],
                            lhsT=s_wout[:, kg * 1024 + og * 128: kg * 1024 + og * 128 + 128],
                            rhs=ybr(tc, kg), start=(kg == 0), stop=(kg == G4 - 1))
                em.inc(last, "mm", 1)
            on("tensor", f_wout_b)
            mm_wb = em.counts["mm"]
            wout_mm[tc] = mm_wb

            def f_wout_eb(e, tc=tc, mm_wb=mm_wb):
                em.wait(e, "scalar", "mm", mm_wb)
                em.wait(e, "scalar", "wst", wo_ev_st[(tc, 0)])
                last = None
                for og in range(4, G8):
                    last = e.activation(out=wv_slot(og - 4), in_=pb[og - 4][:, :],
                                        func=AF.Identity, bias=s_boutg[:, og:og + 1])
                em.inc(last, "sc", 1)
            on("scalar", f_wout_eb)
            sc_web = em.counts["sc"]
            wout_evac_sc[tc] = sc_web

            def f_wout_db(e, tc=tc, sc_web=sc_web):
                em.wait(e, "sync", "sc", sc_web)
                half = tc // 4
                col = (tc % 4) * TC
                for og in range(4, G8):
                    em.inc(e.dma_start(
                        out=wo_in.ap()[half:half + 1, og * 128:(og + 1) * 128, col:col + TC],
                        in_=wv_slot(og - 4)), "wst", 16)
            on("sync", f_wout_db)
            wo_ev_st[(tc, 1)] = em.counts["wst"]
            wo_st.append(em.counts["wst"])

        ve_base = em.counts["ve"]
        _order = []
        for _tc in range(NCH):
            _order.append(("f", _tc))
            if _tc >= 2:
                _order.append(("s", _tc - 2))
        _order += [("s", NCH - 2), ("s", NCH - 1)]
        scan_done_static = {}
        _acc = ve_base
        for kind, k in _order:
            _acc += 7 if kind == "f" else 4
            if kind == "s":
                scan_done_static[k] = _acc

        xbl_x = {}

        def emit_gpx(tcl, npl, smm):
            def f_gpx(e, tcl=tcl, npl=npl, smm=smm):
                if tcl >= 1:
                    em.wait(e, "gpsimd", "mm", smm)
                    em.wait(e, "gpsimd", "ve", npl)
                for g in range(G8):
                    em.inc(e.dma_start(out=xbf_g(g),
                                       in_=xT[g * 128:(g + 1) * 128, tcl * TC:(tcl + 1) * TC]),
                           "xbl", 16)
            on("gpsimd", f_gpx)
            xbl_x[tcl] = em.counts["xbl"]

        emit_gpx(0, 0, 0)

        for tc in range(NCH):
            gld_x = xbl_x[tc]

            def f_statx(e, tc=tc, gld_x=gld_x):
                em.wait(e, "tensor", "xbl", gld_x)
                if tc >= 1:
                    em.wait(e, "tensor", "sc", gate_evac_sc[tc - 1])
                last = None
                for g in range(G8):
                    last = e.matmul(out=pb[4][0:1, :], lhsT=s_onesc[:, :],
                                    rhs=xbf_g(g), start=(g == 0), stop=(g == G8 - 1))
                em.inc(last, "mm", 1)
            on("tensor", f_statx)

            sc_base = em.counts["sc"]
            mm_base = em.counts["mm"]

            def f_sq_all(e, tc=tc, gld_x=gld_x, mm_base=mm_base):
                em.wait(e, "scalar", "xbl", gld_x)
                if tc >= 1:
                    em.wait(e, "scalar", "mm", statmm[tc - 1])
                for g in range(G8):
                    slot = st4b[:, (g % 2) * TC:(g % 2) * TC + TC]
                    if g >= 2:
                        em.wait(e, "scalar", "mm", mm_base + g - 1)
                    em.inc(e.activation(out=slot, in_=xbf_g(g), func=AF.Square),
                           "sc", 1)
            on("scalar", f_sq_all)
            assert em.counts["sc"] == sc_base + G8

            for g in range(G8):
                slot = st4b[:, (g % 2) * TC:(g % 2) * TC + TC]

                def f_sqmm(e, g=g, slot=slot, sc_base=sc_base):
                    em.wait(e, "tensor", "sc", sc_base + g + 1)
                    em.inc(e.matmul(out=pb[5][0:1, :], lhsT=s_onesc[:, :], rhs=slot,
                                    start=(g == 0), stop=(g == G8 - 1)), "mm", 1)
                on("tensor", f_sqmm)
            assert em.counts["mm"] == mm_base + G8
            statmm[tc] = em.counts["mm"]

            ve_r2_pred = em.counts["ve"] + 1

            def f_rows13(e, tc=tc, pmm=prev_mm_rep, pve=prev_ve_r4a,
                         ve_r2_pred=ve_r2_pred):
                em.wait(e, "scalar", "mm", statmm[tc])
                em.wait(e, "scalar", "mm", pmm)
                em.wait(e, "scalar", "ve", pve)
                e.activation(out=rows_m, in_=pb[4][0:1, :], func=AF.Copy, scale=1.0 / D)
                em.inc(e.activation(out=rows_t, in_=pb[5][0:1, :], func=AF.Copy,
                                    scale=1.0 / D), "sc", 1)
                em.wait(e, "scalar", "ve", ve_r2_pred)
                em.inc(e.activation(out=rows_t, in_=rows_t, func=AF.Sqrt,
                                    bias=np.float32(1e-5)), "sc", 1)
            on("scalar", f_rows13)
            sc_r1 = em.counts["sc"] - 1
            sc_r3 = em.counts["sc"]

            def f_rows2(e, sc_r1=sc_r1):
                em.wait(e, "vector", "sc", sc_r1)
                e.tensor_mul(rows_t2, rows_m, rows_m)
                em.inc(e.tensor_sub(rows_t, rows_t, rows_t2), "ve", 1)
            on("vector", f_rows2)
            assert em.counts["ve"] == ve_r2_pred

            def f_rows4a(e, sc_r3=sc_r3):
                em.wait(e, "vector", "sc", sc_r3)
                em.inc(e.reciprocal(out=rows_r, in_=rows_t), "ve", 1)
            on("vector", f_rows4a)
            ve_r4a = em.counts["ve"]
            prev_ve_r4a = ve_r4a

            def f_rep(e, sc_r3=sc_r3, ve_r4a=ve_r4a, tc=tc):
                em.wait(e, "tensor", "sc", sc_r3)
                em.wait(e, "tensor", "ve", ve_r4a)
                if tc >= 1:
                    em.wait(e, "tensor", "ve", norm_prev_last)
                e.matmul(out=pb[6][:, :], lhsT=s_ones1[:, :], rhs=rows_r,
                         start=True, stop=True)
                em.inc(e.matmul(out=pb[7][:, :], lhsT=s_ones1[:, :], rhs=rows_m,
                                start=True, stop=True), "mm", 1)
            on("tensor", f_rep)
            mm_rep = em.counts["mm"]
            prev_mm_rep = mm_rep

            def f_nm(e, tc=tc, mm_rep=mm_rep):
                em.wait(e, "vector", "mm", mm_rep)
                last = None
                for g in range(G8):
                    e.tensor_sub(s_stage[:, :], xbf_g(g), pb[7][:, :])
                    last = e.tensor_mul(ln1_g(g), s_stage[:, :], pb[6][:, :])
                em.inc(last, "ve", 1)
            on("vector", f_nm)
            ve_norm = em.counts["ve"]
            norm_prev_last = ve_norm

            def f_win(e, tc=tc, ve_norm=ve_norm):
                em.wait(e, "tensor", "gld", gld0)
                em.wait(e, "tensor", "ve", ve_norm)
                if tc >= 1:
                    em.wait(e, "tensor", "sc", bcevac_sc[tc - 1])
                if tc >= 4:
                    em.wait(e, "tensor", "sc", wout_evac_sc[tc - 4])
                if tc >= RINGM:
                    em.wait(e, "tensor", "ve", ygate_done[tc - RINGM])
                last = None
                for og in range(G4):
                    e.matmul(out=pb[og][:, :],
                             lhsT=s_binr[0:1, og * 128:(og + 1) * 128],
                             rhs=s_onesr[:, :], start=True, stop=False)
                    for kg in range(G8):
                        last = e.matmul(
                            out=pb[og][:, :],
                            lhsT=bv(O_WIN, 16384)[:, kg * 1024 + og * 128: kg * 1024 + og * 128 + 128],
                            rhs=ln1_g(kg), start=False, stop=(kg == G8 - 1))
                em.inc(last, "mm", 1)
            on("tensor", f_win)
            mm_ssm = em.counts["mm"]

            def f_wing(e, tc=tc):
                last = None
                for og in range(G4):
                    bank = pb[4 + og]
                    e.matmul(out=bank[:, :],
                             lhsT=s_binr[0:1, 512 + og * 128: 512 + og * 128 + 128],
                             rhs=s_onesr[:, :], start=True, stop=False)
                    for kg in range(G8):
                        last = e.matmul(
                            out=bank[:, :],
                            lhsT=bv(O_WIN, 16384)[:, kg * 1024 + 512 + og * 128: kg * 1024 + 512 + og * 128 + 128],
                            rhs=ln1_g(kg), start=False, stop=(kg == G8 - 1))
                em.inc(last, "mm", 1)
            on("tensor", f_wing)
            mm_gate = em.counts["mm"]

            def f_evac1(e, tc=tc, mm_ssm=mm_ssm):
                em.wait(e, "scalar", "mm", mm_ssm)
                if tc >= 1:
                    em.wait(e, "scalar", "ve", conv_done[tc - 1])
                last = None
                for g in range(G4):
                    last = e.activation(out=xs_g(g)[:, 3:515], in_=pb[g][:, :],
                                        func=AF.Copy)
                em.inc(last, "sc", 1)
            on("scalar", f_evac1)
            sc_ev1 = em.counts["sc"]

            def f_evac2(e, tc=tc, mm_gate=mm_gate):
                em.wait(e, "scalar", "mm", mm_gate)
                if tc >= RINGM:
                    em.wait(e, "scalar", "ve", ygate_done[tc - RINGM])
                last = None
                for g in range(G4):
                    last = e.activation(out=gtr(tc, g), in_=pb[4 + g][:, :],
                                        func=AF.Silu)
                em.inc(last, "sc", 1)
            on("scalar", f_evac2)
            gate_evac_sc[tc] = em.counts["sc"]

            for gpair in range(2):
                def f_conv(e, tc=tc, gpair=gpair, sc_ev1=sc_ev1, silu_prev=silu_prev):
                    em.wait(e, "vector", "sc", sc_ev1)
                    em.wait(e, "vector", "sc", silu_prev)
                    for gi in range(2):
                        g = gpair * 2 + gi
                        dst = (s_stage2 if gi == 0 else s_stage3)[:, :]
                        e.tensor_scalar(out=dst, in0=xs_g(g)[:, 0:TC],
                                        scalar1=s_convw[:, g * 4:g * 4 + 1],
                                        scalar2=None, op0=OP.mult)
                        for j in (1, 2, 3):
                            e.scalar_tensor_tensor(
                                out=dst, in0=xs_g(g)[:, j:j + TC],
                                scalar=s_convw[:, g * 4 + j:g * 4 + j + 1],
                                in1=dst, op0=OP.mult, op1=OP.add)
                        em.inc(e.tensor_copy(xs_g(g)[:, 0:3], xs_g(g)[:, TC:TC + 3]),
                               "ve", 1)
                on("vector", f_conv)
                ve_cv = em.counts["ve"]

                def f_silu(e, tc=tc, gpair=gpair, ve_cv=ve_cv):
                    em.wait(e, "scalar", "ve", ve_cv)
                    if tc >= RINGM:
                        em.wait(e, "scalar", "ve", ygate_done[tc - RINGM])
                    for gi in range(2):
                        g = gpair * 2 + gi
                        src = (s_stage2 if gi == 0 else s_stage3)[:, :]
                        em.inc(e.activation(out=xcr(tc, g), in_=src, func=AF.Silu,
                                            bias=s_convb[:, g:g + 1]), "sc", 1)
                on("scalar", f_silu)
                silu_prev = em.counts["sc"]
            conv_done[tc] = em.counts["ve"]
            sc_xc = em.counts["sc"]

            def f_bc(e, tc=tc, sc_xc=sc_xc):
                em.wait(e, "tensor", "sc", sc_xc)
                if tc >= 1:
                    em.wait(e, "tensor", "sc", bcevac_sc[tc - 1])
                last = None
                for g in range(G4):
                    last = e.matmul(out=pb[0][0:32, :],
                                    lhsT=s_wx[:, g * 32:(g + 1) * 32],
                                    rhs=xcr(tc, g), start=(g == 0), stop=(g == G4 - 1))
                em.inc(last, "mm", 1)
            on("tensor", f_bc)
            mm_bc = em.counts["mm"]

            def f_bce(e, tc=tc, mm_bc=mm_bc):
                em.wait(e, "scalar", "mm", mm_bc)
                em.wait(e, "scalar", "bst", bc_st.get(tc - 1, 0))
                em.inc(e.activation(out=s_rows2[:, :], in_=pb[0][0:32, :],
                                    func=AF.Identity, bias=s_bxc[:, 0:1]), "sc", 1)
            on("scalar", f_bce)
            bcevac_sc[tc] = em.counts["sc"]

            def f_bcd(e, tc=tc):
                em.wait(e, "sync", "sc", bcevac_sc[tc])
                em.inc(e.dma_start(out=bc_in[tc].ap(), in_=s_rows2[:, :]), "bst", 16)
            on("sync", f_bcd)
            bc_st[tc] = em.counts["bst"]

            if tc + 1 < NCH:
                emit_gpx(tc + 1, ve_norm, statmm[tc])

            def f_ar(e, tc=tc):
                em.wait(e, "gpsimd", "bst", bc_st[tc])
                em.inc(e.collective_compute(
                    "AllReduce", OP.add, replica_groups=GROUPS,
                    ins=[bc_in[tc].ap().opt()], outs=[bc_out[tc].ap().opt()]), "cc", 1)
            on("gpsimd", f_ar)
            ar_cc[tc] = em.counts["cc"]

            def f_tree(e, tc=tc):
                em.wait(e, "gpsimd", "cc", ar_cc[tc])
                if tc >= 1:
                    em.wait(e, "gpsimd", "mcx", tc)
                em.inc(e.dma_start(
                    out=REP[0:1, :],
                    in_=bc_out[tc].ap().rearrange("a b -> (a b)")[None, :]), "tr", 16)
                nrep = 1
                while nrep < 32:
                    em.wait(e, "gpsimd", "tr")
                    em.inc(e.dma_start(out=REP[nrep:2 * nrep, :],
                                       in_=REP[0:nrep, :]), "tr", 16)
                    nrep *= 2
                em.wait(e, "gpsimd", "tr")
                for j in range(1, 4):
                    em.inc(e.dma_start(out=REP[32 * j:32 * (j + 1), :],
                                       in_=REP[0:32, :]), "tr", 16)
            on("gpsimd", f_tree)
            back_ctx[tc] = (em.counts["tr"], sc_xc)

            if tc >= 2:
                emit_scan(tc - 2)
            if tc >= 3:
                emit_wout(tc - 3)
        for tl in range(NCH, NCH + 3):
            if tl - 2 < NCH:
                emit_scan(tl - 2)
            emit_wout(tl - 3)

        def f_wors(e):
            em.wait(e, "gpsimd", "wst", max(wo_st))
            em.inc(e.collective_compute(
                "ReduceScatter", OP.add, replica_groups=GROUPS,
                ins=[wo_in.ap().opt()], outs=[wo_own.ap().opt()]), "cc", 1)
        on("gpsimd", f_wors)
        cc_wors = em.counts["cc"]

        def f_qkvw(e):
            em.wait(e, "gpsimd", "ve", scan_done[NCH - 1])
            em.wait(e, "gpsimd", "mm", wout_mm[NCH - 1])
            for g in range(G8):
                em.inc(e.dma_start(out=bv(O_QKV, 16384)[:, g * 1024:(g + 1) * 1024],
                                   in_=qkvT[g * 128:(g + 1) * 128, 0:1024]), "wld", 16)
            em.inc(e.dma_start(out=bv(O_GW, 128)[:, 0:64].rearrange("p (g c) -> p g c", g=8),
                               in_=gwT.rearrange("(g p) c -> p g c", p=128)), "wld", 16)
        on("gpsimd", f_qkvw)
        gld_qkvw = em.counts["wld"]



        xaf_g = lambda g, tco: fv(O_XAF + g * TH + tco * TC, TC)
        asm_ve = 0
        for tco in range(NCHO):
            for g in range(G8):
                def f_xal(e, tco=tco, g=g):
                    if asm_ve:
                        em.wait(e, "sync", "ve", asm_ve)
                    em.inc(e.dma_start(out=s_stage4[:, :],
                                       in_=x_ownT[g * 128:(g + 1) * 128, tco * TC:(tco + 1) * TC]),
                           "ld", 16)
                on("sync", f_xal)
                ld_a = em.counts["ld"]

                def f_xwo(e, tco=tco, g=g):
                    em.wait(e, "gpsimd", "cc", cc_wors)
                    if asm_ve:
                        em.wait(e, "gpsimd", "ve", asm_ve)
                    em.inc(e.dma_start(
                        out=st2b[:, 0:TC],
                        in_=wo_own.ap()[0:1, g * 128:(g + 1) * 128, tco * TC:(tco + 1) * TC]),
                        "gld", 16)
                on("gpsimd", f_xwo)
                gld_a = em.counts["gld"]

                def f_xadd(e, tco=tco, g=g, ld_a=ld_a, gld_a=gld_a):
                    em.wait(e, "vector", "ld", ld_a)
                    em.wait(e, "vector", "gld", gld_a)
                    em.inc(e.tensor_add(xaf_g(g, tco), s_stage4[:, :], st2b[:, 0:TC]),
                           "ve", 1)
                on("vector", f_xadd)
                asm_ve = em.counts["ve"]
        asm_done = em.counts["ve"]

        z_g = lambda g, tco: fv(O_Z + g * TH + tco * TC, TC)
        r1_st = []
        ln2_ve = 0
        prev_ve_5d = 0
        for tco in range(NCHO):
            sqmm = []
            for g in range(G8):
                xslot = st3b[:, (g % 2) * TC:(g % 2) * TC + TC]
                sslot = st4b[:, (g % 2) * TC:(g % 2) * TC + TC]

                pq = "pqA" if g % 2 == 0 else "pqB"

                def f_cast(e, tco=tco, g=g, xslot=xslot, pq=pq):
                    em.wait(e, "gpsimd", "ve", asm_done)
                    if g >= 2:
                        em.wait(e, "gpsimd", "mm", sqmm[g - 2][0])
                    em.inc(e.dma_start(out=xslot, in_=xaf_g(g, tco)), pq, 16)
                on("gpsimd", f_cast)
                gld_c = em.counts[pq]

                def f_sq5(e, tco=tco, g=g, xslot=xslot, sslot=sslot, gld_c=gld_c, pq=pq):
                    em.wait(e, "scalar", pq, gld_c)
                    if g >= 2:
                        em.wait(e, "scalar", "mm", sqmm[g - 2][1])
                    em.inc(e.activation(out=sslot, in_=xslot, func=AF.Square), "sc", 1)
                on("scalar", f_sq5)
                sc_q = em.counts["sc"]

                def f_mm5(e, tco=tco, g=g, xslot=xslot, sslot=sslot, sc_q=sc_q,
                          gld_c=gld_c, pq=pq):
                    em.wait(e, "tensor", pq, gld_c)
                    em.wait(e, "tensor", "sc", sc_q)
                    if tco == 0 and g == 0:
                        em.wait(e, "tensor", "wld", gld_qkvw)
                    m1 = e.matmul(out=pb[4][0:1, :], lhsT=s_onesc[:, :], rhs=xslot,
                                  start=(g == 0), stop=(g == G8 - 1))
                    em.inc(m1, "mm", 1)
                    m2 = e.matmul(out=pb[5][0:1, :], lhsT=s_onesc[:, :], rhs=sslot,
                                  start=(g == 0), stop=(g == G8 - 1))
                    em.inc(m2, "mm", 1)
                on("tensor", f_mm5)
                sqmm.append((em.counts["mm"] - 1, em.counts["mm"]))
            mm_s5 = em.counts["mm"]

            def f_rows5a(e, mm_s5=mm_s5, prev_ve_5d=prev_ve_5d):
                em.wait(e, "scalar", "mm", mm_s5)
                em.wait(e, "scalar", "gp", prev_ve_5d)
                e.activation(out=rows_m, in_=pb[4][0:1, :], func=AF.Copy, scale=1.0 / D)
                em.inc(e.activation(out=rows_t, in_=pb[5][0:1, :], func=AF.Copy,
                                    scale=1.0 / D), "sc", 1)
            on("scalar", f_rows5a)
            sc_5a = em.counts["sc"]

            def f_rows5b(e, sc_5a=sc_5a):
                em.wait(e, "gpsimd", "sc", sc_5a)
                e.tensor_mul(rows_t2, rows_m, rows_m)
                em.inc(e.tensor_sub(rows_t, rows_t, rows_t2), "gp", 1)
            on("gpsimd", f_rows5b)
            gp_5b = em.counts["gp"]

            def f_rows5c(e, gp_5b=gp_5b):
                em.wait(e, "scalar", "gp", gp_5b)
                em.inc(e.activation(out=rows_t, in_=rows_t, func=AF.Sqrt,
                                    bias=np.float32(1e-5)), "sc", 1)
            on("scalar", f_rows5c)
            sc_5c = em.counts["sc"]

            def f_rows5da(e, sc_5c=sc_5c):
                em.wait(e, "vector", "sc", sc_5c)
                em.inc(e.reciprocal(out=rows_r, in_=rows_t), "ve", 1)
            on("vector", f_rows5da)
            ve_5da = em.counts["ve"]

            def f_rows5d(e, ve_5da=ve_5da):
                em.wait(e, "gpsimd", "ve", ve_5da)
                em.inc(e.tensor_mul(rows_mr, rows_m, rows_r), "gp", 1)
            on("gpsimd", f_rows5d)
            ve_5d = em.counts["gp"]
            prev_ve_5d = ve_5d

            def f_rep5(e, ve_5d=ve_5d, tco=tco):
                em.wait(e, "tensor", "gp", ve_5d)
                if ln2_ve:
                    em.wait(e, "tensor", "ve", ln2_ve)
                e.matmul(out=pb[6][:, :], lhsT=s_ones1[:, :], rhs=rows_r,
                         start=True, stop=True)
                em.inc(e.matmul(out=pb[7][:, :], lhsT=s_ones1[:, :], rhs=rows_mr,
                                start=True, stop=True), "mm", 1)
            on("tensor", f_rep5)
            mm_rep5 = em.counts["mm"]

            def f_norm5(e, tco=tco, mm_rep5=mm_rep5):
                em.wait(e, "vector", "mm", mm_rep5)
                last = None
                for g in range(G8):
                    e.tensor_mul(s_stage[:, :], xaf_g(g, tco), pb[6][:, :])
                    last = e.tensor_sub(z_g(g, tco), s_stage[:, :], pb[7][:, :])
                em.inc(last, "ve", 1)
            on("vector", f_norm5)
            ln2_ve = em.counts["ve"]

            sq2mm = []
            for g in range(G8):
                sslot = st4b[:, (g % 2) * TC:(g % 2) * TC + TC]

                def f_sqr(e, tco=tco, g=g, sslot=sslot, ln2_ve=ln2_ve):
                    em.wait(e, "scalar", "ve", ln2_ve)
                    if g >= 2:
                        em.wait(e, "scalar", "mm", sq2mm[g - 2])
                    em.inc(e.activation(out=sslot, in_=z_g(g, tco), func=AF.Square),
                           "sc", 1)
                on("scalar", f_sqr)
                sc_q2 = em.counts["sc"]

                def f_mmr(e, g=g, sslot=sslot, sc_q2=sc_q2):
                    em.wait(e, "tensor", "sc", sc_q2)
                    em.inc(e.matmul(out=pb[4][0:1, :], lhsT=s_onesc[:, :], rhs=sslot,
                                    start=(g == 0), stop=(g == G8 - 1)), "mm", 1)
                on("tensor", f_mmr)
                sq2mm.append(em.counts["mm"])
            mm_r1 = em.counts["mm"]

            def f_rs1(e, mm_r1=mm_r1):
                em.wait(e, "scalar", "mm", mm_r1)
                em.inc(e.activation(out=rows_t, in_=pb[4][0:1, :], func=AF.Sqrt,
                                    scale=1.0 / D, bias=np.float32(1e-6)), "sc", 1)
            on("scalar", f_rs1)
            sc_rs1 = em.counts["sc"]

            def f_rs1v(e, sc_rs1=sc_rs1):
                em.wait(e, "vector", "sc", sc_rs1)
                em.inc(e.reciprocal(out=rows_r, in_=rows_t), "ve", 1)
            on("vector", f_rs1v)
            ve_rs1 = em.counts["ve"]

            def f_repr1(e, ve_rs1=ve_rs1):
                em.wait(e, "tensor", "ve", ve_rs1)
                em.inc(e.matmul(out=pb[6][:, :], lhsT=s_ones1[:, :], rhs=rows_r,
                                start=True, stop=True), "mm", 1)
            on("tensor", f_repr1)
            mm_repr1 = em.counts["mm"]

            for g in range(G8):
                slot = st2b[:, (g % 2) * TC:(g % 2) * TC + TC]

                def f_r1n(e, tco=tco, g=g, slot=slot, mm_repr1=mm_repr1):
                    em.wait(e, "vector", "mm", mm_repr1)
                    if len(r1_st) >= 2:
                        em.wait(e, "vector", "rst", r1_st[-1])
                    em.inc(e.scalar_tensor_tensor(
                        out=slot, in0=z_g(g, tco), scalar=s_n1wg[:, g:g + 1],
                        in1=pb[6][:, :], op0=OP.mult, op1=OP.mult), "ve", 1)
                on("vector", f_r1n)
                ve_r1n = em.counts["ve"]

                def f_r1d(e, tco=tco, g=g, slot=slot, ve_r1n=ve_r1n):
                    em.wait(e, "sync", "ve", ve_r1n)
                    em.inc(e.dma_start(
                        out=rms1_in.ap()[0:1, g * 128:(g + 1) * 128, tco * TC:(tco + 1) * TC],
                        in_=slot), "rst", 16)
                on("sync", f_r1d)
                r1_st.append(em.counts["rst"])
        rms1_last_st = em.counts["rst"]

        def f_pageout(e):
            em.wait(e, "sync", "ve", ln2_ve)
            for g in range(G8):
                em.inc(e.dma_start(out=xaf_d.ap()[g * 128:(g + 1) * 128, :],
                                   in_=fv(O_XAF + g * TH, TH)), "st", 16)
        on("sync", f_pageout)
        st_pageout = em.counts["st"]

        def f_qvw(e):
            em.wait(e, "gpsimd", "st", st_pageout)
            for g in range(G8):
                em.inc(e.dma_start(out=bv(O_QKVV, 4096)[:, g * 512:(g + 1) * 512],
                                   in_=qkvT[g * 128:(g + 1) * 128, 1024:1536]), "wld", 16)
        on("gpsimd", f_qvw)
        gld_qvw = em.counts["wld"]


        def f_ag(e):
            em.wait(e, "gpsimd", "rst", rms1_last_st)
            em.inc(e.collective_compute(
                "AllGather", OP.bypass, replica_groups=GROUPS,
                ins=[rms1_in.ap().opt()], outs=[rms1_out.ap().opt()]), "cc", 1)
        on("gpsimd", f_ag)
        cc_ag = em.counts["cc"]

        RMS1C = bv(O_RMS1C, 4096)
        qkv_ev_st = 0
        qst = {}
        for tc in range(NCH):
            def f_r1l(e, tc=tc):
                em.wait(e, "gpsimd", "cc", cc_ag)
                em.wait(e, "gpsimd", "st", st_pageout)
                if tc >= 1:
                    em.wait(e, "gpsimd", "mm", vmm_last)
                slot = tc // 4
                col = (tc % 4) * TC
                for g in range(G8):
                    em.inc(e.dma_start(
                        out=RMS1C[:, g * TC:(g + 1) * TC],
                        in_=rms1_out.ap()[slot:slot + 1, g * 128:(g + 1) * 128, col:col + TC]),
                        "gld", 16)
            on("gpsimd", f_r1l)
            gld_r1 = em.counts["gld"]

            def f_qk(e, tc=tc, gld_r1=gld_r1):
                em.wait(e, "tensor", "gld", gld_r1)
                em.wait(e, "tensor", "wld", gld_qkvw)
                if tc >= 1:
                    em.wait(e, "tensor", "sc", sc_kev)
                    em.wait(e, "tensor", "sc", sc_vev)
                last = None
                for og in range(G4):
                    for kg in range(G8):
                        last = e.matmul(
                            out=pb[og][:, :],
                            lhsT=bv(O_QKV, 16384)[:, kg * 1024 + og * 128: kg * 1024 + og * 128 + 128],
                            rhs=RMS1C[:, kg * TC:(kg + 1) * TC],
                            start=(kg == 0), stop=(kg == G8 - 1))
                em.inc(last, "mm", 1)
                last = None
                for og in range(G4):
                    for kg in range(G8):
                        last = e.matmul(
                            out=pb[4 + og][:, :],
                            lhsT=bv(O_QKV, 16384)[:, kg * 1024 + 512 + og * 128: kg * 1024 + 512 + og * 128 + 128],
                            rhs=RMS1C[:, kg * TC:(kg + 1) * TC],
                            start=(kg == 0), stop=(kg == G8 - 1))
                em.inc(last, "mm", 1)
            sc_kev = qst.get("sc_kev", 0)
            sc_vev = qst.get("sc_vev", 0)
            on("tensor", f_qk)
            mm_q = em.counts["mm"] - 1
            mm_k = em.counts["mm"]

            def f_qev(e, tc=tc, mm_q=mm_q):
                em.wait(e, "scalar", "mm", mm_q)
                em.wait(e, "scalar", "st", qst.get("v_st", 0))
                last = None
                for og in range(G4):
                    dst = (st2b if og % 2 == 0 else st3b)[:, (og // 2) * TC:(og // 2) * TC + TC]
                    last = e.activation(out=dst, in_=pb[og][:, :], func=AF.Copy)
                em.inc(last, "sc", 1)
            on("scalar", f_qev)
            sc_qev = em.counts["sc"]

            q2mms = []
            for og in range(G4):
                def f_sqog(e, tc=tc, og=og):
                    if og >= 2:
                        em.wait(e, "scalar", "mm", q2mms[og - 2])
                    src_ = (st2b if og % 2 == 0 else st3b)[:, (og // 2) * TC:(og // 2) * TC + TC]
                    em.inc(e.activation(out=st4b[:, (og % 2) * TC:(og % 2) * TC + TC],
                                        in_=src_, func=AF.Square), "sc", 1)
                on("scalar", f_sqog)
                sc_og = em.counts["sc"]

                def f_q2og(e, tc=tc, og=og, sc_og=sc_og):
                    em.wait(e, "tensor", "sc", sc_og)
                    em.inc(e.matmul(out=pb[og][0:2, :], lhsT=s_hsel[:, :],
                                    rhs=st4b[:, (og % 2) * TC:(og % 2) * TC + TC],
                                    start=True, stop=True), "mm", 1)
                on("tensor", f_q2og)
                q2mms.append(em.counts["mm"])
            mm_q2 = em.counts["mm"]

            def f_q2ev(e, tc=tc, mm_q2=mm_q2):
                em.wait(e, "vector", "mm", mm_q2)
                last = None
                for og in range(G4):
                    last = e.tensor_reduce(out=s_g8[0:2, og * 8 + tc: og * 8 + tc + 1],
                                           in_=pb[og][0:2, :], axis=AX.X, op=OP.max)
                em.inc(last, "ve", 1)
            on("vector", f_q2ev)
            sc_q2ev = em.counts["ve"]

            def f_qdma(e, tc=tc, sc_qev=sc_qev):
                em.wait(e, "sync", "sc", sc_qev)
                for og in range(G4):
                    src = (st2b if og % 2 == 0 else st3b)[:, (og // 2) * TC:(og // 2) * TC + TC]
                    em.inc(e.dma_start(
                        out=qQ.ap()[og * 128:(og + 1) * 128, tc * TC:(tc + 1) * TC],
                        in_=src), "st", 16)
            on("sync", f_qdma)
            qst["q_st"] = em.counts["st"]

            def f_kev(e, tc=tc, mm_k=mm_k):
                em.wait(e, "scalar", "mm", mm_k)
                em.wait(e, "scalar", "st", qst["q_st"])
                last = None
                for og in range(G4):
                    dst = (st2b if og % 2 == 0 else st3b)[:, (og // 2) * TC:(og // 2) * TC + TC]
                    last = e.activation(out=dst, in_=pb[4 + og][:, :], func=AF.Copy)
                em.inc(last, "sc", 1)
            on("scalar", f_kev)
            qst["sc_kev"] = sc_kev2 = em.counts["sc"]

            def f_kmax(e, tc=tc, sc_kev2=sc_kev2):
                em.wait(e, "vector", "sc", sc_kev2)
                for og in range(G4):
                    src = (st2b if og % 2 == 0 else st3b)[:, (og // 2) * TC:(og // 2) * TC + TC]
                    e.tensor_reduce(out=s_stage[:, og:og + 1], in_=src, axis=AX.X,
                                    op=OP.max, apply_absolute_value=True)
                last = None
                for og in range(G4):
                    last = e.tensor_max(s_kmx[:, og:og + 1], s_kmx[:, og:og + 1],
                                        s_stage[:, og:og + 1])
                em.inc(last, "ve", 1)
            on("vector", f_kmax)
            ve_km = em.counts["ve"]

            def f_kdma(e, tc=tc, ve_km=ve_km):
                em.wait(e, "sync", "ve", ve_km)
                for og in range(G4):
                    src = (st2b if og % 2 == 0 else st3b)[:, (og // 2) * TC:(og // 2) * TC + TC]
                    em.inc(e.dma_start(
                        out=qK.ap()[og * 128:(og + 1) * 128, tc * TC:(tc + 1) * TC],
                        in_=src), "st", 16)
            on("sync", f_kdma)
            qst["k_st"] = em.counts["st"]

            def f_v(e, tc=tc, sc_q2ev=sc_q2ev):
                em.wait(e, "tensor", "ve", sc_q2ev)
                em.wait(e, "tensor", "wld", gld_qvw)
                for ts in range(4):
                    last = None
                    for kg in range(G8):
                        last = e.matmul(
                            out=pb[ts][:, :],
                            lhsT=RMS1C[:, kg * TC + ts * 128: kg * TC + ts * 128 + 128],
                            rhs=bv(O_QKVV, 4096)[:, kg * 512:(kg + 1) * 512],
                            start=(kg == 0), stop=(kg == G8 - 1))
                    em.inc(last, "mm", 1)
            on("tensor", f_v)
            vmm_last = em.counts["mm"]

            def f_vev(e, tc=tc, vmm_last=vmm_last):
                em.wait(e, "scalar", "mm", vmm_last)
                em.wait(e, "scalar", "st", qst["k_st"])
                last = None
                for ts in range(4):
                    dst = (st2b if ts % 2 == 0 else st3b)[:, (ts // 2) * TC:(ts // 2) * TC + TC]
                    last = e.activation(out=dst, in_=pb[ts][:, :], func=AF.Copy)
                em.inc(last, "sc", 1)
            on("scalar", f_vev)
            qst["sc_vev"] = sc_vev2 = em.counts["sc"]

            def f_vdma(e, tc=tc, sc_vev2=sc_vev2):
                em.wait(e, "sync", "sc", sc_vev2)
                for ts in range(4):
                    src = (st2b if ts % 2 == 0 else st3b)[:, (ts // 2) * TC:(ts // 2) * TC + TC]
                    em.inc(e.dma_start(
                        out=qV.ap()[tc * TC + ts * 128: tc * TC + (ts + 1) * 128, :],
                        in_=src), "st", 16)
            on("sync", f_vdma)
            qst["v_st"] = em.counts["st"]
        qkv_done_st = em.counts["st"]
        qkv_done_ve = em.counts["ve"]

        def f_c1a(e):
            em.wait(e, "gpsimd", "ve", qkv_done_ve)
            em.inc(e.dma_start(out=ROWS[0:1, 0:512], in_=s_kmx[:, :]), "tr", 16)
            for hh in range(2):
                em.inc(e.dma_start(
                    out=ROWS[0:1, 520:584].rearrange("a (og hh2 tc) -> a og hh2 tc", og=4, hh2=2)[:, :, hh, :],
                    in_=s_g8[hh:hh + 1, 0:32], allow_slow_non_contiguous=True), "tr", 16)
        on("gpsimd", f_c1a)
        gld_c1a = em.counts["tr"]

        def f_c1b(e, gld_c1a=gld_c1a):
            em.wait(e, "vector", "tr", gld_c1a)
            em.inc(e.tensor_reduce(
                out=ROWS[0:1, 512:520].rearrange("a (kg hh) -> a kg hh", kg=4),
                in_=ROWS[0:1, 0:512].rearrange("a (hh i kg) -> a kg hh i", hh=2, i=64, kg=4),
                axis=AX.X, op=OP.max), "ve", 1)
        on("vector", f_c1b)
        ve_c1b = em.counts["ve"]

        def f_c1d(e, ve_c1b=ve_c1b):
            em.wait(e, "scalar", "ve", ve_c1b)
            e.mul(out=ROWS[0:1, 512:520], in_=ROWS[0:1, 512:520], mul=-0.125)
            em.inc(e.activation(out=ROWS[0:1, 520:584], in_=ROWS[0:1, 520:584],
                                func=AF.Sqrt), "sc", 1)
        on("scalar", f_c1d)
        sc_c1d = em.counts["sc"]

        def f_c1e(e, sc_c1d=sc_c1d):
            em.wait(e, "gpsimd", "sc", sc_c1d)
            e.tensor_tensor(
                out=ROWS[0:1, 520:584].rearrange("a (h tc) -> a h tc", h=8),
                in0=ROWS[0:1, 520:584].rearrange("a (h tc) -> a h tc", h=8),
                in1=ROWS[0:1, 512:520].unsqueeze(2).to_broadcast([1, 8, 8]),
                op=OP.mult)
            em.inc(e.tensor_scalar(out=ROWS[0:1, 520:584], in0=ROWS[0:1, 520:584],
                                   scalar1=75.0, scalar2=None, op0=OP.add), "gp", 1)
        on("gpsimd", f_c1e)
        ve_c1e = em.counts["gp"]

        def f_c1f(e, ve_c1e=ve_c1e):
            em.wait(e, "gpsimd", "gp", ve_c1e)
            em.inc(e.dma_start(out=s_c1r[0:1, :], in_=ROWS[0:1, 520:584]), "tr", 16)
            nrep = 1
            while nrep < 128:
                em.wait(e, "gpsimd", "tr")
                em.inc(e.dma_start(out=s_c1r[nrep:2 * nrep, :],
                                   in_=s_c1r[0:nrep, :]), "tr", 16)
                nrep *= 2
        on("gpsimd", f_c1f)
        gld_c1f = em.counts["tr"]

        def f_oww(e):
            em.wait(e, "gpsimd", "mm", vmm_last)
            for g in range(G4):
                em.inc(e.dma_start(out=bv(O_OW, 8192)[:, g * 1024:(g + 1) * 1024],
                                   in_=owT[g * 128:(g + 1) * 128, :]), "wld", 16)
        on("gpsimd", f_oww)
        gld_ow = em.counts["wld"]

        KP = bv(O_KP, 4096)
        VT = [bv(O_VT0, 2080), bv(O_VT1, 2080)]
        QT = bv(O_QT, 512)
        PT = [bv(O_P0, 1024), bv(O_P1, 1024)]
        AVv = bv(O_AV, 1024)
        AVN = bv(O_AVN, 512)
        RATT = bv(O_RATT, 16384)

        att_gp = {}
        qt_mm_last = 0
        att_prev_ve = 0
        att_prev_mm = 0
        att_prev_pv = 0
        att_prev_exp = 0
        att_prev_sc = 0
        for hp in range(G4):
            def f_kvload(e, hp=hp):
                em.wait(e, "gpsimd", "tr", gld_c1f)
                em.wait(e, "gpsimd", "st", qkv_done_st)
                em.wait(e, "gpsimd", "mm", att_prev_mm if hp >= 1 else vmm_last)
                em.inc(e.dma_start(out=KP[:, :], in_=qK.ap()[hp * 128:(hp + 1) * 128, :]),
                       "kvl", 16)
                for hh in range(2):
                    h0 = 64 * (2 * hp + hh)
                    em.inc(e.memset(VT[hh][:, :], 1.0), "kvl", 1)
                    for q in range(4):
                        em.inc(e.dma_start(
                            out=VT[hh].rearrange("p (kt j) -> p kt j", j=65)[:, q * 8:(q + 1) * 8, 0:64],
                            in_=qV.ap().rearrange("(kt p) d -> p kt d", p=128)[:, q * 8:(q + 1) * 8, h0:h0 + 64]),
                            "kvl", 16)
            on("gpsimd", f_kvload)
            gld_kv = em.counts["kvl"]

            for qc in range(NCH):
                def f_ql(e, hp=hp, qc=qc, qt_prev=None):
                    em.wait(e, "sync", "mm", qt_mm_last)
                    em.inc(e.dma_start(out=QT[:, :],
                                       in_=qQ.ap()[hp * 128:(hp + 1) * 128, qc * TC:(qc + 1) * TC]),
                           "ld", 16)
                on("sync", f_ql)
                ld_q = em.counts["ld"]

                ktmax = min(NKT - 1, 4 * qc + 3)
                nkts = ktmax + 1
                prev_pv = att_prev_pv
                prev_exp = att_prev_exp
                mm0 = em.counts["mm"]
                sc0 = em.counts["sc"]
                ve0 = em.counts["ve"]
                smm_p = {kt: mm0 + 2 * kt + 1 for kt in range(nkts)}
                pv_p = {kt: mm0 + 2 * kt + 2 for kt in range(nkts)}
                mm_rc = mm0 + 2 * nkts + 1
                exp_p = {kt: sc0 + kt + 1 for kt in range(nkts)}
                sc_av = sc0 + nkts + 1
                pz_kts = [kt for kt in range(nkts) if kt * 128 - qc * TC > 0]
                msets = {kt: ve0 + i + 1 for i, kt in enumerate(pz_kts)}
                ve_rc = ve0 + len(pz_kts) + 1
                ve_at = ve_rc + 1

                def f_qc_t(e, hp=hp, qc=qc, ktmax=ktmax, ld_q=ld_q, gld_kv=gld_kv,
                           prev_exp=prev_exp, exp_p=exp_p, msets=msets, ve_rc=ve_rc):
                    em.wait(e, "tensor", "ld", ld_q)
                    em.wait(e, "tensor", "kvl", gld_kv)
                    for kt in range(ktmax + 1):
                        j0 = max(0, kt * 128 - qc * TC)
                        bank0 = pb[0] if kt % 2 == 0 else pb[6]
                        bank1 = pb[1] if kt % 2 == 0 else pb[7]
                        boundary = (kt >= 4 * qc)
                        em.wait(e, "tensor", "sc", exp_p.get(kt - 2, prev_exp))
                        e.matmul(out=bank0[:, j0:TC],
                                 lhsT=KP[0:64, kt * 128:(kt + 1) * 128],
                                 rhs=QT[0:64, j0:TC],
                                 start=True, stop=not boundary,
                                 tile_position=(0, 0), skip_group_check=True)
                        m1 = e.matmul(out=bank1[:, j0:TC],
                                      lhsT=KP[64:128, kt * 128:(kt + 1) * 128],
                                      rhs=QT[64:128, j0:TC],
                                      start=True, stop=not boundary,
                                      tile_position=(64, 0), skip_group_check=True)
                        last = m1
                        if boundary:
                            e.matmul(out=bank0[:, j0:j0 + 128], lhsT=s_triu[:, :],
                                     rhs=s_negi[:, :], start=False, stop=True,
                                     skip_group_check=True)
                            last = e.matmul(out=bank1[:, j0:j0 + 128],
                                            lhsT=s_triu[:, :], rhs=s_negi[:, :],
                                            start=False, stop=True,
                                            skip_group_check=True)
                        em.inc(last, "mm", 1)
                        em.wait(e, "tensor", "sc", exp_p[kt])
                        if kt in msets:
                            em.wait(e, "tensor", "ve", msets[kt])
                        if kt == 0 and (qc >= 1 or hp >= 1):
                            em.wait(e, "tensor", "ve", att_prev_ve)
                        e.matmul(out=pb[2][0:65, :],
                                 lhsT=VT[0][:, kt * 65:(kt + 1) * 65],
                                 rhs=PT[0][:, (kt % 2) * TC:(kt % 2) * TC + TC],
                                 start=(kt == 0), stop=(kt == ktmax),
                                 skip_group_check=True)
                        em.inc(e.matmul(out=pb[3][0:65, :],
                                        lhsT=VT[1][:, kt * 65:(kt + 1) * 65],
                                        rhs=PT[1][:, (kt % 2) * TC:(kt % 2) * TC + TC],
                                        start=(kt == 0), stop=(kt == ktmax),
                                        skip_group_check=True), "mm", 1)
                    em.wait(e, "tensor", "ve", ve_rc)
                    e.matmul(out=pb[4][:, :], lhsT=s_ones1[:, :],
                             rhs=ROWS[0:1, 1024:1536], start=True, stop=True)
                    em.inc(e.matmul(out=pb[5][:, :], lhsT=s_ones1[:, :],
                                    rhs=ROWS[0:1, 1536:2048], start=True, stop=True),
                           "mm", 1)
                on("tensor", f_qc_t)
                assert em.counts["mm"] == mm_rc, (em.counts["mm"], mm_rc)

                def f_qc_s(e, hp=hp, qc=qc, ktmax=ktmax, smm_p=smm_p, pv_p=pv_p,
                           prev_pv=prev_pv):
                    c0 = (2 * hp) * 8 + qc
                    c1c = (2 * hp + 1) * 8 + qc
                    for kt in range(ktmax + 1):
                        j0 = max(0, kt * 128 - qc * TC)
                        bank0 = pb[0] if kt % 2 == 0 else pb[6]
                        bank1 = pb[1] if kt % 2 == 0 else pb[7]
                        em.wait(e, "scalar", "mm", smm_p[kt])
                        em.wait(e, "scalar", "mm", pv_p.get(kt - 2, prev_pv))
                        e.activation(out=PT[0][:, (kt % 2) * TC + j0:(kt % 2) * TC + TC],
                                     in_=bank0[:, j0:TC], func=AF.Exp, scale=0.125,
                                     bias=s_c1r[:, c0:c0 + 1])
                        em.inc(e.activation(out=PT[1][:, (kt % 2) * TC + j0:(kt % 2) * TC + TC],
                                            in_=bank1[:, j0:TC], func=AF.Exp,
                                            scale=0.125, bias=s_c1r[:, c1c:c1c + 1]),
                               "sc", 1)
                    em.wait(e, "scalar", "mm", pv_p[ktmax])
                    e.activation(out=s_stage[0:64, :], in_=pb[2][0:64, :], func=AF.Copy)
                    em.inc(e.activation(out=s_stage2[0:64, :], in_=pb[3][0:64, :],
                                        func=AF.Copy), "sc", 1)
                on("scalar", f_qc_s)
                assert em.counts["sc"] == sc_av, (em.counts["sc"], sc_av)

                def f_qc_v(e, hp=hp, qc=qc, pz_kts=pz_kts, pv_p=pv_p, prev_pv=prev_pv,
                           sc_av=sc_av, mm_rc=mm_rc):
                    for kt in pz_kts:
                        j0 = kt * 128 - qc * TC
                        em.wait(e, "vector", "mm", pv_p.get(kt - 2, prev_pv))
                        e.memset(PT[0][:, (kt % 2) * TC:(kt % 2) * TC + j0], 0.0)
                        em.inc(e.memset(PT[1][:, (kt % 2) * TC:(kt % 2) * TC + j0], 0.0),
                               "ve", 1)
                    em.wait(e, "vector", "mm", pv_p[max(pv_p)])
                    e.reciprocal(out=ROWS[0:1, 1024:1536], in_=pb[2][64:65, :])
                    em.inc(e.reciprocal(out=ROWS[0:1, 1536:2048], in_=pb[3][64:65, :]),
                           "ve", 1)
                    em.wait(e, "vector", "sc", sc_av)
                    em.wait(e, "vector", "mm", mm_rc)
                    if att_gp.get("avn_gld"):
                        em.wait(e, "vector", "gld", att_gp["avn_gld"])
                    e.tensor_tensor(out=RATT[0:64, hp * T + qc * TC: hp * T + (qc + 1) * TC],
                                    in0=s_stage[0:64, :], in1=pb[4][0:64, :], op=OP.mult)
                    em.inc(e.tensor_tensor(out=AVN[0:64, :], in0=s_stage2[0:64, :],
                                           in1=pb[5][0:64, :], op=OP.mult), "ve", 1)
                on("vector", f_qc_v)
                assert em.counts["ve"] == ve_at, (em.counts["ve"], ve_at)

                qt_mm_last = pv_p[ktmax]
                att_prev_pv = pv_p[ktmax]
                att_prev_exp = exp_p[ktmax]
                att_prev_sc = sc_av
                att_prev_ve = ve_at

                def f_avmv(e, hp=hp, qc=qc):
                    em.wait(e, "gpsimd", "ve", att_prev_ve)
                    em.inc(e.dma_start(
                        out=RATT[64:128, hp * T + qc * TC: hp * T + (qc + 1) * TC],
                        in_=AVN[0:64, :]), "gld", 16)
                on("gpsimd", f_avmv)
                att_gp["avn_gld"] = em.counts["gld"]
            att_prev_mm = em.counts["mm"]
        attn_done_gld = att_gp["avn_gld"]
        attn_done_ve = att_prev_ve

        osts = []
        o_ev_st = 0
        for tc in range(NCH):
            def f_omm_a(e, tc=tc):
                em.wait(e, "tensor", "wld", gld_ow)
                em.wait(e, "tensor", "gld", attn_done_gld)
                em.wait(e, "tensor", "ve", attn_done_ve)
                if tc >= 1:
                    em.wait(e, "tensor", "sc", o_sc[(tc - 1, 1)])
                last = None
                for og in range(G4):
                    for kg in range(G4):
                        last = e.matmul(
                            out=pb[og][:, :],
                            lhsT=bv(O_OW, 8192)[:, kg * 1024 + og * 128: kg * 1024 + og * 128 + 128],
                            rhs=RATT[:, kg * T + tc * TC: kg * T + (tc + 1) * TC],
                            start=(kg == 0), stop=(kg == G4 - 1))
                em.inc(last, "mm", 1)
            o_sc = {} if tc == 0 else o_sc
            on("tensor", f_omm_a)
            mm_oa = em.counts["mm"]

            def f_oev_a(e, tc=tc, mm_oa=mm_oa):
                em.wait(e, "scalar", "mm", mm_oa)
                em.wait(e, "scalar", "wst", o_st.get((tc - 1, 1), 0))
                last = None
                for og in range(G4):
                    dst = (st2b if og % 2 == 0 else st3b)[:, (og // 2) * TC:(og // 2) * TC + TC]
                    last = e.activation(out=dst, in_=pb[og][:, :], func=AF.Copy)
                em.inc(last, "sc", 1)
            o_st = {} if tc == 0 else o_st
            on("scalar", f_oev_a)
            o_sc[(tc, 0)] = em.counts["sc"]

            def f_odma_a(e, tc=tc):
                em.wait(e, "sync", "sc", o_sc[(tc, 0)])
                half = tc // 4
                col = (tc % 4) * TC
                for og in range(G4):
                    src = (st2b if og % 2 == 0 else st3b)[:, (og // 2) * TC:(og // 2) * TC + TC]
                    em.inc(e.dma_start(
                        out=op_in.ap()[half:half + 1, og * 128:(og + 1) * 128, col:col + TC],
                        in_=src), "wst", 16)
            on("sync", f_odma_a)
            o_st[(tc, 0)] = em.counts["wst"]

            def f_omm_b(e, tc=tc):
                em.wait(e, "tensor", "sc", o_sc[(tc, 0)])
                last = None
                for og in range(4, G8):
                    for kg in range(G4):
                        last = e.matmul(
                            out=pb[og - 4][:, :],
                            lhsT=bv(O_OW, 8192)[:, kg * 1024 + og * 128: kg * 1024 + og * 128 + 128],
                            rhs=RATT[:, kg * T + tc * TC: kg * T + (tc + 1) * TC],
                            start=(kg == 0), stop=(kg == G4 - 1))
                em.inc(last, "mm", 1)
            on("tensor", f_omm_b)
            mm_ob = em.counts["mm"]

            def f_oev_b(e, tc=tc, mm_ob=mm_ob):
                em.wait(e, "scalar", "mm", mm_ob)
                em.wait(e, "scalar", "wst", o_st[(tc, 0)])
                last = None
                for og in range(4, G8):
                    dst = (st2b if og % 2 == 0 else st3b)[:, ((og - 4) // 2) * TC:((og - 4) // 2) * TC + TC]
                    last = e.activation(out=dst, in_=pb[og - 4][:, :], func=AF.Copy)
                em.inc(last, "sc", 1)
            on("scalar", f_oev_b)
            o_sc[(tc, 1)] = em.counts["sc"]

            def f_odma_b(e, tc=tc):
                em.wait(e, "sync", "sc", o_sc[(tc, 1)])
                half = tc // 4
                col = (tc % 4) * TC
                for og in range(4, G8):
                    src = (st2b if og % 2 == 0 else st3b)[:, ((og - 4) // 2) * TC:((og - 4) // 2) * TC + TC]
                    em.inc(e.dma_start(
                        out=op_in.ap()[half:half + 1, og * 128:(og + 1) * 128, col:col + TC],
                        in_=src), "wst", 16)
            on("sync", f_odma_b)
            o_st[(tc, 1)] = em.counts["wst"]
            osts.append(em.counts["wst"])

        def f_ors(e):
            em.wait(e, "gpsimd", "wst", max(osts))
            em.inc(e.collective_compute(
                "ReduceScatter", OP.add, replica_groups=GROUPS,
                ins=[op_in.ap().opt()], outs=[op_own.ap().opt()]), "cc", 1)
        on("gpsimd", f_ors)
        cc_ors = em.counts["cc"]

        zup_ve = 0
        for tco in range(NCHO):
            for g in range(G8):
                def f_ol(e, tco=tco, g=g):
                    em.wait(e, "gpsimd", "cc", cc_ors)
                    if zup_ve:
                        em.wait(e, "gpsimd", "ve", zup_ve)
                    em.inc(e.dma_start(
                        out=st2b[:, 0:TC],
                        in_=op_own.ap()[0:1, g * 128:(g + 1) * 128, tco * TC:(tco + 1) * TC]),
                        "gld", 16)
                on("gpsimd", f_ol)
                gld_o = em.counts["gld"]

                def f_zadd(e, tco=tco, g=g, gld_o=gld_o):
                    em.wait(e, "vector", "gld", gld_o)
                    em.inc(e.tensor_add(z_g(g, tco), z_g(g, tco), st2b[:, 0:TC]),
                           "ve", 1)
                on("vector", f_zadd)
                zup_ve = em.counts["ve"]
        zup_done = em.counts["ve"]

        RMS2C = bv(O_RMS2C, 4096)
        HIDv = bv(O_HID, 16384)
        FROW = fv(O_FROW, 2048)
        out_st = []
        fin_st = []
        up_gld = {}
        ffn_prev = {}
        for tco in range(NCHO):
            def f_xafl(e, tco=tco):
                em.wait(e, "sync", "st", st_pageout)
                em.wait(e, "sync", "ve", zup_done)
                if tco >= 1:
                    em.wait(e, "sync", "ve", ffn_prev.get("final_ve", 0))
                for g in range(G8):
                    em.inc(e.dma_start(out=fv(O_XAFC + g * TC, TC),
                                       in_=xaf_d.ap()[g * 128:(g + 1) * 128, tco * TC:(tco + 1) * TC]),
                           "ld", 16)
            on("sync", f_xafl)
            ld_xafc = em.counts["ld"]

            sq3 = []
            for g in range(G8):
                sslot = st4b[:, (g % 2) * TC:(g % 2) * TC + TC]

                def f_sq8(e, tco=tco, g=g, sslot=sslot):
                    em.wait(e, "scalar", "ve", zup_done)
                    if g >= 2:
                        em.wait(e, "scalar", "mm", sq3[g - 2])
                    elif tco >= 1:
                        em.wait(e, "scalar", "ve", ffn_prev["dn_ve_last"])
                        em.wait(e, "scalar", "ve", ffn_prev["final_ve"])
                    em.inc(e.activation(out=sslot, in_=z_g(g, tco), func=AF.Square),
                           "sc", 1)
                on("scalar", f_sq8)
                sc_s8 = em.counts["sc"]

                def f_mm8(e, g=g, sslot=sslot, sc_s8=sc_s8, tco=tco):
                    em.wait(e, "tensor", "sc", sc_s8)
                    if tco >= 1 and g == 0:
                        em.wait(e, "tensor", "sc", ffn_prev["dnev_sc"])
                    em.inc(e.matmul(out=pb[4][0:1, :], lhsT=s_onesc[:, :], rhs=sslot,
                                    start=(g == 0), stop=(g == G8 - 1)), "mm", 1)
                on("tensor", f_mm8)
                sq3.append(em.counts["mm"])
            mm_s8 = em.counts["mm"]

            def f_rs2(e, mm_s8=mm_s8):
                em.wait(e, "scalar", "mm", mm_s8)
                em.inc(e.activation(out=rows_t, in_=pb[4][0:1, :], func=AF.Sqrt,
                                    scale=1.0 / D, bias=np.float32(1e-6)), "sc", 1)
            on("scalar", f_rs2)
            sc_rs2 = em.counts["sc"]

            def f_rs2v(e, sc_rs2=sc_rs2):
                em.wait(e, "vector", "sc", sc_rs2)
                em.inc(e.reciprocal(out=rows_r, in_=rows_t), "ve", 1)
            on("vector", f_rs2v)
            ve_rs2 = em.counts["ve"]

            def f_repr2(e, ve_rs2=ve_rs2):
                em.wait(e, "tensor", "ve", ve_rs2)
                em.inc(e.matmul(out=pb[6][:, :], lhsT=s_ones1[:, :], rhs=rows_r,
                                start=True, stop=True), "mm", 1)
            on("tensor", f_repr2)
            mm_repr2 = em.counts["mm"]

            def f_r2n(e, tco=tco, mm_repr2=mm_repr2):
                em.wait(e, "vector", "mm", mm_repr2)
                last = None
                for g in range(G8):
                    last = e.scalar_tensor_tensor(
                        out=RMS2C[:, g * TC:(g + 1) * TC], in0=z_g(g, tco),
                        scalar=s_n2wg[:, g:g + 1], in1=pb[6][:, :],
                        op0=OP.mult, op1=OP.mult)
                em.inc(last, "ve", 1)
            on("vector", f_r2n)
            ve_r2n = em.counts["ve"]

            gl_sc = {}
            L = lambda lane, a, b: s_g8[:, lane * 32 + a:lane * 32 + b]
            for ts in range(4):
                def f_gl(e, tco=tco, ts=ts, ve_r2n=ve_r2n):
                    em.wait(e, "tensor", "ve", ve_r2n)
                    if ts >= 1:
                        em.wait(e, "tensor", "sc", gl_sc[ts - 1])
                    last = None
                    for kg in range(G8):
                        last = e.matmul(
                            out=pb[5][0:128, 0:8],
                            lhsT=RMS2C[:, kg * TC + ts * 128: kg * TC + ts * 128 + 128],
                            rhs=bv(O_GW, 128)[:, kg * 8:(kg + 1) * 8],
                            start=(kg == 0), stop=(kg == G8 - 1))
                    em.inc(last, "mm", 1)
                on("tensor", f_gl)
                mm_gl = em.counts["mm"]

                def f_gle(e, ts=ts, mm_gl=mm_gl, tco=tco):
                    em.wait(e, "scalar", "mm", mm_gl)
                    if ts == 0 and tco >= 1:
                        em.wait(e, "scalar", "ve", ffn_prev["gtop_ve"])
                    em.inc(e.activation(out=L(ts, 0, 8), in_=pb[5][0:128, 0:8],
                                        func=AF.Copy), "sc", 1)
                on("scalar", f_gle)
                gl_sc[ts] = em.counts["sc"]

            def f_gmax(e, tco=tco):
                em.wait(e, "vector", "sc", gl_sc[3])
                for lane in range(4):
                    e.tensor_reduce(out=L(lane, 16, 17), in_=L(lane, 0, 8),
                                    axis=AX.X, op=OP.max)
                last = None
                for lane in range(4):
                    last = e.tensor_sub(L(lane, 17, 18), s_gc[:, 0:1], L(lane, 16, 17))
                em.inc(last, "ve", 1)
            on("vector", f_gmax)
            ve_gm = em.counts["ve"]

            def f_gexp(e, tco=tco, ve_gm=ve_gm):
                em.wait(e, "scalar", "ve", ve_gm)
                last = None
                for lane in range(4):
                    last = e.activation(out=L(lane, 8, 16), in_=L(lane, 0, 8),
                                        func=AF.Exp, bias=L(lane, 17, 18),
                                        accum_out=L(lane, 18, 19))
                em.inc(last, "sc", 1)
            on("scalar", f_gexp)
            sc_ge = em.counts["sc"]

            def f_gtop(e, tco=tco, sc_ge=sc_ge):
                em.wait(e, "vector", "sc", sc_ge)
                for lane in range(4):
                    e.tensor_tensor(out=L(lane, 20, 28), in0=L(lane, 8, 16),
                                    in1=s_gc[:, 2:3].to_broadcast([128, 8]),
                                    op=OP.is_ge)
                for lane in range(4):
                    e.tensor_add(L(lane, 20, 28), L(lane, 20, 28), L(lane, 20, 28))
                for lane in range(4):
                    e.tensor_sub(L(lane, 20, 28), L(lane, 8, 16), L(lane, 20, 28))
                for lane in range(4):
                    e.tensor_reduce(out=L(lane, 28, 29), in_=L(lane, 20, 28),
                                    axis=AX.X, op=OP.max)
                for lane in range(4):
                    e.reciprocal(out=L(lane, 19, 20), in_=L(lane, 18, 19))
                for lane in range(4):
                    e.tensor_add(L(lane, 28, 29), s_gc[:, 2:3], L(lane, 28, 29))
                for lane in range(4):
                    e.tensor_mul(L(lane, 29, 30), L(lane, 28, 29), L(lane, 19, 20))
                for lane in range(4):
                    e.tensor_add(L(lane, 30, 31), L(lane, 29, 30), s_gc[:, 1:2])
                for lane in range(4):
                    e.reciprocal(out=L(lane, 31, 32), in_=L(lane, 30, 31))
                last = None
                for lane in range(4):
                    last = e.tensor_mul(s_fcol[:, lane:lane + 1],
                                        L(lane, 29, 30), L(lane, 31, 32))
                em.inc(last, "ve", 1)
            on("vector", f_gtop)
            ffn_prev["gtop_ve"] = em.counts["ve"]
            ve_f = em.counts["ve"]

            def f_fd(e, tco=tco, ve_f=ve_f):
                em.wait(e, "gpsimd", "ve", ve_f)
                for ts in range(4):
                    em.inc(e.dma_start(
                        out=FROW[0:1, tco * TC + ts * 128: tco * TC + (ts + 1) * 128],
                        in_=s_fcol[:, ts:ts + 1],
                        allow_slow_non_contiguous=True), "tr", 16)
            on("gpsimd", f_fd)
            gld_f = em.counts["tr"]

            hev_sc = {}
            for hs in range(2):
                def f_upl(e, tco=tco, hs=hs):
                    if tco >= 1 or hs >= 1:
                        em.wait(e, "gpsimd", "mm", up_gld.get("mm_prev", 0))
                    for kg in range(G8):
                        em.inc(e.dma_start(
                            out=bv(O_UP0 + kg * 1024, 2048),
                            in_=upT[kg * 128:(kg + 1) * 128, hs * 2048:(hs + 1) * 2048]),
                            "wld", 16)
                on("gpsimd", f_upl)
                gld_up = em.counts["wld"]

                for hb in range(2):
                    def f_hmm(e, tco=tco, hs=hs, hb=hb, gld_up=gld_up, ve_r2n=ve_r2n):
                        em.wait(e, "tensor", "wld", gld_up)
                        em.wait(e, "tensor", "ve", ve_r2n)
                        if (hs, hb) != (0, 0):
                            em.wait(e, "tensor", "sc", hev_sc[prev_hb])
                        else:
                            em.wait(e, "tensor", "sc", gl_sc[3])
                        for hgi in range(8):
                            last = None
                            for kg in range(G8):
                                last = e.matmul(
                                    out=pb[hgi][:, :],
                                    lhsT=bv(O_UP0 + kg * 1024, 2048)[:, hb * 1024 + hgi * 128: hb * 1024 + hgi * 128 + 128],
                                    rhs=RMS2C[:, kg * TC:(kg + 1) * TC],
                                    start=(kg == 0), stop=(kg == G8 - 1))
                            em.inc(last, "mm", 1)
                    on("tensor", f_hmm)
                    mm_h = em.counts["mm"]

                    def f_hev(e, tco=tco, hs=hs, hb=hb, mm_h=mm_h):
                        em.wait(e, "scalar", "mm", mm_h)
                        last = None
                        for hgi in range(8):
                            hg = hs * 16 + hb * 8 + hgi
                            last = e.activation(out=HIDv[:, hg * TC:(hg + 1) * TC],
                                                in_=pb[hgi][:, :], func=AF.Silu)
                        em.inc(last, "sc", 1)
                    on("scalar", f_hev)
                    hev_sc[(hs, hb)] = em.counts["sc"]
                    prev_hb = (hs, hb)
                up_gld["mm_prev"] = em.counts["mm"]
            hid_sc = em.counts["sc"]

            def f_freprep(e, tco=tco, gld_f=gld_f):
                em.wait(e, "tensor", "tr", gld_f)
                em.wait(e, "tensor", "sc", hid_sc)
                em.inc(e.matmul(out=pb[7][:, :], lhsT=s_ones1[:, :],
                                rhs=FROW[0:1, tco * TC:(tco + 1) * TC],
                                start=True, stop=True), "mm", 1)
            on("tensor", f_freprep)
            mm_fr = em.counts["mm"]

            def f_frev(e, mm_fr=mm_fr):
                em.wait(e, "scalar", "mm", mm_fr)
                em.inc(e.activation(out=s_stage3[:, :], in_=pb[7][:, :], func=AF.Copy),
                       "sc", 1)
            on("scalar", f_frev)
            sc_fr = em.counts["sc"]

            dn_mm = {}
            for kh in range(32):
                pqd = "pqA" if kh % 2 == 0 else "pqB"

                def f_dnl(e, tco=tco, kh=kh, pqd=pqd):
                    if kh >= 2 or tco >= 1:
                        em.wait(e, "gpsimd", "mm", dn_mm.get(kh - 2, ffn_prev.get("dn_mm_last", 0)))
                    em.inc(e.dma_start(
                        out=bv(O_DN0 if kh % 2 == 0 else O_DN1, 1024),
                        in_=dnT[kh * 128:(kh + 1) * 128, :]), pqd, 16)
                on("gpsimd", f_dnl)
                gld_dn = em.counts[pqd]

                def f_dmm(e, tco=tco, kh=kh, gld_dn=gld_dn, hid_sc=hid_sc, pqd=pqd):
                    em.wait(e, "tensor", pqd, gld_dn)
                    em.wait(e, "tensor", "sc", hid_sc)
                    if kh == 0:
                        em.wait(e, "tensor", "sc", sc_fr)
                    last = None
                    for og in range(G8):
                        last = e.matmul(
                            out=pb[og][:, :],
                            lhsT=bv(O_DN0 if kh % 2 == 0 else O_DN1, 1024)[:, og * 128:(og + 1) * 128],
                            rhs=HIDv[:, kh * TC:(kh + 1) * TC],
                            start=(kh == 0), stop=(kh == 31))
                    em.inc(last, "mm", 1)
                on("tensor", f_dmm)
                dn_mm[kh] = em.counts["mm"]
            ffn_prev["dn_mm_last"] = dn_mm[31]

            dn_sc_last = 0
            for og in range(G8):
                def f_dev(e, tco=tco, og=og):
                    em.wait(e, "scalar", "mm", dn_mm[31])
                    em.wait(e, "scalar", "ve", dn_ve.get(og - 2, 0))
                    if og < 2:
                        em.wait(e, "scalar", "st", ffn_prev.get("fin_st", 0))
                    em.inc(e.activation(out=st2b[:, (og % 2) * TC:(og % 2) * TC + TC],
                                        in_=pb[og][:, :], func=AF.Copy), "sc", 1)
                dn_ve = {} if og == 0 else dn_ve
                on("scalar", f_dev)
                sc_d = em.counts["sc"]

                def f_dz(e, tco=tco, og=og, sc_d=sc_d):
                    em.wait(e, "vector", "sc", sc_d)
                    e.tensor_tensor(out=s_stage4[:, :],
                                    in0=st2b[:, (og % 2) * TC:(og % 2) * TC + TC],
                                    in1=s_stage3[:, :], op=OP.mult)
                    em.inc(e.tensor_add(z_g(og, tco), z_g(og, tco), s_stage4[:, :]),
                           "ve", 1)
                on("vector", f_dz)
                dn_ve[og] = em.counts["ve"]
            ffn_prev["dnev_sc"] = em.counts["sc"]
            ffn_prev["dn_ve_last"] = dn_ve[G8 - 1]

            for g in range(G8):
                def f_fin(e, tco=tco, g=g, ld_xafc=ld_xafc):
                    em.wait(e, "vector", "ld", ld_xafc)
                    em.wait(e, "vector", "ve", dn_ve[G8 - 1])
                    if len(fin_st) >= 2:
                        em.wait(e, "vector", "st", fin_st[-2])
                    em.inc(e.tensor_add((s_stage if g % 2 == 0 else s_stage2)[:, :],
                                        fv(O_XAFC + g * TC, TC), z_g(g, tco)), "ve", 1)
                on("vector", f_fin)
                ve_fin = em.counts["ve"]

                def f_find(e, tco=tco, g=g, ve_fin=ve_fin):
                    em.wait(e, "sync", "ve", ve_fin)
                    em.inc(e.dma_start(
                        out=out_fm[g * 128:(g + 1) * 128, tco * TC:(tco + 1) * TC],
                        in_=(s_stage if g % 2 == 0 else s_stage2)[:, :]), "st", 16)
                on("sync", f_find)
                fin_st.append(em.counts["st"])
                out_st.append(em.counts["st"])
            ffn_prev["final_ve"] = em.counts["ve"]
            ffn_prev["fin_st"] = fin_st[-1]

        def f_tail(e):
            em.wait(e, "sync", "st", max(out_st))
        on("sync", f_tail)

    return nc


_NC_CACHE = {}


def _get_nc():
    if "nc" not in _NC_CACHE:
        _NC_CACHE["nc"] = build_program()
    return _NC_CACHE["nc"]


def kernel(L0_attn_x, L0_attn_W_in, L0_attn_b_in, L0_attn_conv_w, L0_attn_conv_b,
           L0_attn_W_x, L0_attn_b_x, L0_attn_A_log, L0_attn_D_param,
           L0_attn_W_out, L0_attn_b_out, L0_ffn_n1_w, L0_ffn_qkv_w, L0_ffn_o_w,
           L0_ffn_n2_w, L0_ffn_gate_w, L0_ffn_up_w, L0_ffn_down_w):
    f32 = np.float32
    x = np.asarray(L0_attn_x, f32)
    W_in = np.asarray(L0_attn_W_in, f32)
    b_in = np.asarray(L0_attn_b_in, f32)
    conv_w = np.asarray(L0_attn_conv_w, f32)
    conv_b = np.asarray(L0_attn_conv_b, f32)
    W_x = np.asarray(L0_attn_W_x, f32)
    b_x = np.asarray(L0_attn_b_x, f32)
    A_log = np.asarray(L0_attn_A_log, f32)
    D_param = np.asarray(L0_attn_D_param, f32)
    W_out = np.asarray(L0_attn_W_out, f32)
    b_out = np.asarray(L0_attn_b_out, f32)
    n1_w = np.asarray(L0_ffn_n1_w, f32)
    qkv_w = np.asarray(L0_ffn_qkv_w, f32)
    o_w = np.asarray(L0_ffn_o_w, f32)
    n2_w = np.asarray(L0_ffn_n2_w, f32)
    gate_w = np.asarray(L0_ffn_gate_w, f32)
    up_w = np.asarray(L0_ffn_up_w, f32)
    down_w = np.asarray(L0_ffn_down_w, f32)

    def grp(v, ng):
        return np.ascontiguousarray(v.reshape(ng, 128).T)

    qkvT_full = np.ascontiguousarray(qkv_w.T)
    owT_full = np.ascontiguousarray(o_w.T)
    upT_full = np.ascontiguousarray(up_w.T)
    dnT_full = np.ascontiguousarray(down_w.T)
    gwT_full = np.ascontiguousarray(gate_w.T)
    triu_full = np.ascontiguousarray(np.triu(np.ones((128, 128), f32), 1))
    negeye_full = np.ascontiguousarray(-1e9 * np.eye(128, dtype=f32))

    in_maps = []
    for c in range(8):
        b = c // 2
        h = c % 2
        sl = slice(512 * h, 512 * h + 512)
        xTb = np.ascontiguousarray(x[b].T)
        x_ownT = np.ascontiguousarray(x[b, 2048 * h:2048 * h + 2048, :].T)
        winT_own = np.ascontiguousarray(
            np.concatenate([W_in[:, sl], W_in[:, 1024 + 512 * h:1024 + 512 * h + 512]], axis=1))
        binr_own = np.ascontiguousarray(
            np.concatenate([b_in[sl], b_in[1024 + 512 * h:1024 + 512 * h + 512]])[None, :])
        convw_own = np.ascontiguousarray(
            conv_w[sl].reshape(4, 128, 4).transpose(1, 0, 2).reshape(128, 16))
        convb_own = np.ascontiguousarray(conv_b[sl].reshape(4, 128).T)
        alog_own = np.ascontiguousarray(
            A_log[sl].reshape(4, 128, 16).transpose(1, 0, 2).reshape(128, 64))
        dpar_own = np.ascontiguousarray(D_param[sl].reshape(4, 128).T)
        wx_own = np.ascontiguousarray(W_x[sl, :])
        wout_own = np.ascontiguousarray(W_out[sl, :])
        qkvT_own = np.ascontiguousarray(np.concatenate(
            [qkvT_full[:, sl], qkvT_full[:, 1024 + 512 * h:1024 + 512 * h + 512],
             qkvT_full[:, 2048 + 512 * h:2048 + 512 * h + 512]], axis=1))
        owT_own = np.ascontiguousarray(owT_full[sl, :])
        in_maps.append({
            "xT": xTb,
            "x_ownT": x_ownT,
            "winT": winT_own,
            "binr": binr_own,
            "convw": convw_own,
            "convb": convb_own,
            "alog": alog_own,
            "dparg": dpar_own,
            "wxT": wx_own,
            "bxc": np.ascontiguousarray(b_x[:, None]),
            "woutT": wout_own,
            "boutg": grp(b_out, 8),
            "n1wg": grp(n1_w, 8),
            "n2wg": grp(n2_w, 8),
            "qkvT": qkvT_own,
            "owT": owT_own,
            "gwT": gwT_full,
            "upT": upT_full,
            "dnT": dnT_full,
            "triu1": triu_full,
            "negeye": negeye_full,
        })

    nc = _get_nc()
    res = run_bass_kernel_spmd(nc, in_maps, core_ids=list(range(8)))

    out = np.empty((4, T, D), f32)
    for c in range(8):
        b = c // 2
        h = c % 2
        shard = np.asarray(res.results[c]["out_fm"])
        out[b, 2048 * h:2048 * h + 2048, :] = shard.T
    return out
